# version 01 (final)
# baseline (speedup 1.0000x reference)
import numpy as np
import ml_dtypes
from contextlib import ExitStack
import concourse.bass as bass
import concourse.mybir as mybir
from concourse.bass_utils import run_bass_kernel_spmd

F32 = mybir.dt.float32
BF16 = mybir.dt.bfloat16
ALU = mybir.AluOpType
ACT = mybir.ActivationFunctionType
AX = mybir.AxisListType.X

D = 1024
T = 2048
DI = 2048
NST = 16
DTR = 64
DC = D // 128
CC = DI // 128
NE = 8
CAP = 640
FF = 4096
EPS = 1e-5
NTB = T // 512

BF = ml_dtypes.bfloat16
ENGS = ("tensor", "vector", "scalar", "gpsimd", "sync")


class Prog:
    def __init__(self, nc, block, ctx):
        self.nc = nc
        self.block = block
        self.ctx = ctx
        self.sems = {}
        self.nbar = 0

    def sem(self, name):
        if name not in self.sems:
            self._semidx = getattr(self, "_semidx", 0) + 1
            h = self.nc.alloc_semaphore(name=f"{name}_{self._semidx}")
            self.sems[name] = [h, 0]
        return self.sems[name]

    def recycle(self, keep=("BAR",)):
        self.barrier()
        victims = [n for n in self.sems if n not in keep]
        handles = [self.sems[n][0] for n in victims]

        def f(eng):
            self.nc.clear_and_free_semaphores(handles)
        self.block.gpsimd(f)
        for n in victims:
            del self.sems[n]
        self.barrier()

    def inc(self, instr, name, n=1):
        s = self.sem(name)
        instr.then_inc(s[0], n)
        s[1] += n
        return instr

    def expect(self, name):
        return self.sem(name)[1]

    def wait(self, eng, name, val=None):
        s = self.sem(name)
        v = s[1] if val is None else val
        if v > 0:
            eng.wait_ge(s[0], v)

    def barrier(self):
        b = self.block
        s = self.sem("BAR")
        self.nbar += 1
        target = 5 * self.nbar
        for name in ENGS:
            def mk():
                def f(eng):
                    self.inc(eng.drain(), "BAR")
                    eng.wait_ge(s[0], target)
                return f
            getattr(b, name)(mk())
        assert s[1] == target

    def pipe(self, nsteps, stages):
        dmastages = {st["name"]: st for st in stages if st.get("dma")}
        cum = {}
        for nm, st in dmastages.items():
            c0, c1 = [0], [0]
            for i in range(nsteps):
                nd = st["dma"](i)
                c0.append(c0[-1] + (16 * nd if i % 2 == 0 else 0))
                c1.append(c1[-1] + (16 * nd if i % 2 == 1 else 0))
            cum[nm] = (c0, c1)
        base = {}
        def ensure(nm):
            if nm in dmastages:
                for par in (0, 1):
                    pn = nm + "@" + str(par)
                    self.sem(pn)
                    if pn not in base:
                        base[pn] = self.expect(pn)
            else:
                self.sem(nm)
                if nm not in base:
                    base[nm] = self.expect(nm)
        for st in stages:
            ensure(st["name"])
            for dep, fn in st.get("after", ()):
                ensure(dep)
        by_eng = {}
        for st in stages:
            by_eng.setdefault(st["eng"], []).append(st)
        for engname, stgs in by_eng.items():
            def mk(stgs):
                def f(eng):
                    for i in range(nsteps):
                        for st in stgs:
                            for dep, fn in st.get("after", ()):
                                v = fn(i)
                                if dep in dmastages:
                                    k = max(0, min(v, nsteps))
                                    for par in (0, 1):
                                        cv = cum[dep][par][k]
                                        pn = dep + "@" + str(par)
                                        if cv > 0 or base[pn] > 0:
                                            self.wait(eng, pn, base[pn] + cv)
                                else:
                                    if v > 0 or (v == 0 and base[dep] > 0):
                                        self.wait(eng, dep, base[dep] + v)
                            r = st["emit"](eng, i)
                            if st.get("dma"):
                                nd = st["dma"](i)
                                rl = r if isinstance(r, list) else ([] if r is None else [r])
                                assert len(rl) == nd, (st["name"], i, len(rl), nd)
                                pn = st["name"] + "@" + str(i % 2)
                                for instr in rl:
                                    self.inc(instr, pn, 16)
                            elif r is None:
                                self.inc(eng.nop(), st["name"], 1)
                            else:
                                self.inc(r, st["name"], 1)
                    for st in stgs:
                        if st.get("dma"):
                            for par in (0, 1):
                                pn = st["name"] + "@" + str(par)
                                if self.expect(pn) > 0:
                                    eng.wait_ge(self.sems[pn][0], self.expect(pn))
                return f
            getattr(self.block, engname)(mk(stgs))



def host_prep(core_id, inp):
    t = lambda a: np.ascontiguousarray(np.asarray(a).T)
    c = lambda a: np.ascontiguousarray(np.asarray(a))
    m = {}
    m["x_fm"] = t(inp["L0_attn_a_x"][core_id])
    for pre, tag in (("f", "f"), ("b", "b")):
        g = lambda k: inp[f"L0_attn_a_{pre}_{k}"]
        iw = np.asarray(g("in_w")).reshape(32, 128, 8, 128)
        m[f"{tag}_inwTt"] = c(iw.transpose(0, 3, 2, 1))
        ow = np.asarray(g("out_w")).reshape(8, 128, 16, 128)
        m[f"{tag}_outwTt"] = c(ow.transpose(3, 2, 0, 1))
        xp = np.asarray(g("xproj_w")).reshape(96, 16, 128)
        m[f"{tag}_xprojwt"] = c(xp.transpose(2, 1, 0))
        dw = np.asarray(g("dt_w")).reshape(16, 128, 64)
        m[f"{tag}_dtwt"] = c(dw.transpose(2, 0, 1))
        m[f"{tag}_convw"] = c(np.asarray(g("conv_w")).reshape(DI, 4))
        m[f"{tag}_convb"] = c(np.asarray(g("conv_b")).reshape(DI, 1))
        m[f"{tag}_dtb"] = c(np.asarray(g("dt_b")).reshape(DI, 1))
        m[f"{tag}_Alog"] = c(g("A_log"))
        m[f"{tag}_Dvec"] = c(np.asarray(g("D")).reshape(DI, 1))
    m["gamma"] = c(np.asarray(inp["L0_attn_a_gamma"]).reshape(D, 1))
    m["beta"] = c(np.asarray(inp["L0_attn_a_beta"]).reshape(D, 1))
    m["WAT"] = t(inp["L0_attn_b_W_A"])
    m["WVT"] = t(inp["L0_attn_b_W_V"])
    m["WOT"] = t(inp["L0_attn_b_W_O"])
    m["bO"] = c(np.asarray(inp["L0_attn_b_b_O"]).reshape(D, 1))
    m["gate_w"] = c(inp["L0_ffn_gate_w"])
    m["gate_b"] = c(np.asarray(inp["L0_ffn_gate_b"]).reshape(1, NE))
    e = core_id
    m["w1e"] = c(inp["L0_ffn_w1"][e])
    m["b1e"] = c(np.asarray(inp["L0_ffn_b1"][e]).reshape(FF, 1))
    m["w2e"] = c(inp["L0_ffn_w2"][e])
    m["b2e"] = c(np.asarray(inp["L0_ffn_b2"][e]).reshape(D, 1))

    m["ident32"] = np.eye(128, dtype=np.float32)
    m["anti32"] = np.eye(128, dtype=np.float32)[::-1].copy()
    S = np.zeros((CC, 128, 128), dtype=np.float32)
    for j in range(CC):
        for cp in range(8):
            for n in range(NST):
                S[j, cp * 16 + n, 8 * j + cp] = 1
    m["Ssel"] = S
    m["iota8c"] = np.broadcast_to(np.arange(8, dtype=np.float32).reshape(1, 8),
                                  (128, 8)).copy()
    m["iota128rep"] = np.broadcast_to(np.arange(128, dtype=np.float32).reshape(1, 128), (128, 128)).copy()
    m["iotaslot"] = np.arange(5120, dtype=np.float32).reshape(40, 128).T.copy()
    pm = np.zeros((64, 8), dtype=np.float32)
    for r in range(core_id):
        for ee in range(8):
            pm[8 * r + ee, ee] = 1
    m["prefmask"] = pm
    m["ones_col"] = np.ones((128, 1), dtype=np.float32)
    m["ones8"] = np.ones((8, 1), dtype=np.float32)
    m["iota8m"] = np.broadcast_to(8.0 - np.arange(8, dtype=np.float32).reshape(1, 8),
                                  (128, 8)).copy()
    m["iota8col"] = np.arange(8, dtype=np.float32).reshape(8, 1)
    m["gwt"] = c(np.asarray(inp["L0_ffn_gate_w"]).reshape(8, 128, 8))
    m["gbrep"] = np.broadcast_to(np.asarray(inp["L0_ffn_gate_b"]).reshape(1, 8),
                                 (128, 8)).astype(np.float32).copy()
    m["w1t"] = c(np.asarray(m["w1e"]).reshape(8, 128, 32, 128).transpose(2, 1, 0, 3))
    m["b1c"] = c(np.asarray(m["b1e"]).reshape(32, 128).T)
    m["w2t"] = c(np.asarray(m["w2e"]).reshape(32, 128, 1024).transpose(1, 0, 2))
    m["b2rep"] = np.broadcast_to(np.asarray(m["b2e"]).reshape(1, 1024), (128, 1024)).astype(np.float32).copy()
    m["wvtt"] = c(np.asarray(m["WVT"]).reshape(8, 128, 1024).transpose(1, 0, 2))
    m["watt"] = c(np.asarray(m["WAT"]).reshape(8, 128, 16).transpose(1, 0, 2))
    m["wott"] = c(np.asarray(m["WOT"]).reshape(8, 128, 8, 128).transpose(1, 0, 2, 3))
    m["bOc"] = c(np.asarray(m["bO"]).reshape(8, 128).T)
    return m


IO_SHAPES = {
    "x_fm": ([D, T], F32),
    "gamma": ([D, 1], F32), "beta": ([D, 1], F32),
    "WAT": ([D, 16], F32), "WVT": ([D, D], F32), "WOT": ([D, D], F32),
    "bO": ([D, 1], F32),
    "gate_w": ([D, NE], F32), "gate_b": ([1, NE], F32),
    "w1e": ([D, FF], F32), "b1e": ([FF, 1], F32),
    "w2e": ([FF, D], F32), "b2e": ([D, 1], F32),
    "ident32": ([128, 128], F32), "anti32": ([128, 128], F32),
    "Ssel": ([CC, 128, 128], F32),
    "iota8c": ([128, 8], F32),
    "iota128rep": ([128, 128], F32), "iotaslot": ([128, 40], F32),
    "prefmask": ([64, 8], F32),
    "ones_col": ([128, 1], F32),
    "ones8": ([8, 1], F32),
    "iota8m": ([128, 8], F32),
    "iota8col": ([8, 1], F32),
    "gwt": ([8, 128, 8], F32),
    "gbrep": ([128, 8], F32),
    "w1t": ([32, 128, 8, 128], F32),
    "b1c": ([128, 32], F32),
    "w2t": ([128, 32, 1024], F32),
    "b2rep": ([128, 1024], F32),
    "wvtt": ([128, 8, 1024], F32),
    "watt": ([128, 8, 16], F32),
    "wott": ([128, 8, 8, 128], F32),
    "bOc": ([128, 8], F32),
}
for _tag in ("f", "b"):
    IO_SHAPES.update({
        f"{_tag}_inwTt": ([32, 128, DC, 128], F32),
        f"{_tag}_outwTt": ([128, CC, DC, 128], F32),
        f"{_tag}_xprojwt": ([128, CC, 96], F32),
        f"{_tag}_dtwt": ([DTR, CC, 128], F32),
        f"{_tag}_convw": ([DI, 4], F32), f"{_tag}_convb": ([DI, 1], F32),
        f"{_tag}_dtb": ([DI, 1], F32), f"{_tag}_Alog": ([DI, NST], F32),
        f"{_tag}_Dvec": ([DI, 1], F32),
    })


class Bufs:
    pass


def declare_io(nc, dbg_names=()):
    io = Bufs()
    for k, (shp, dt) in IO_SHAPES.items():
        setattr(io, k, nc.dram_tensor(k, shp, dt, kind="ExternalInput").ap())
    io.out = nc.dram_tensor("out", [D, T], F32, kind="ExternalOutput").ap()
    io.dbg = {}
    for name, shp, dt in dbg_names:
        io.dbg[name] = nc.dram_tensor(name, list(shp), dt, kind="ExternalOutput").ap()
    return io


def declare_scratch(nc):
    s = Bufs()
    dram = lambda n, shp, dt: nc.dram_tensor(n, shp, dt, kind="Internal").ap()
    s.s1 = dram("s1d", [D, T], F32)
    s.outdir = dram("outdird", [D, T], F32)
    s.xn_rev = dram("xn_revd", [D, T], BF16)
    s.u_act = dram("u_actd", [DI, T], BF16)
    s.silu_z = dram("silu_zd", [DI, T], BF16)
    s.y_tot = dram("y_totd", [DI, T], BF16)
    s.rowrep = dram("rowrepd", [8, T], F32)
    s.bcrep = dram("bcrepd", [2 * NST, T], BF16)
    s.xcur = dram("xcurd", [D, T], F32)
    s.outdir2 = dram("outdir2d", [D, T], F32)
    s.outdir2r = dram("outdir2rd", [D, T], F32)
    s.xn_d = dram("xn_dd", [D, T], BF16)
    s.xcur2 = dram("xcur2d", [D, T], F32)
    s.rowbounce = dram("rowbounced", [4, T], F32)
    s.disp = dram("dispd", [40 * 128, D], BF16)
    s.rs_out = dram("rs_outd", [CAP, D], BF16)
    s.eo = dram("eod", [CAP, D], BF16)
    s.eoall = dram("eoalld", [NE * CAP, D], BF16)
    s.cc_in = dram("cc_ind", [8, 1], F32)
    s.cc_out = dram("cc_outd", [64, 1], F32)
    s.wp = dram("wpd", [16, 2304], BF16)
    s.dbgr = dram("dbgrd", [16, T], F32)
    s.dbgr2 = dram("dbgr2d", [128, 64], F32)
    s.dbgr3 = dram("dbgr3d", [64, 4], F32)
    return s



def alloc_common(p, io):
    nc, e = p.nc, p.ctx.enter_context
    c = Bufs()
    c.ones_col = e(nc.sbuf_tensor("c_ones", [128, 1], F32))
    c.ident32 = e(nc.sbuf_tensor("c_id32", [128, 128], F32))
    c.identbf = e(nc.sbuf_tensor("identbf", [128, 128], BF16))
    c.antibf = e(nc.sbuf_tensor("antibf", [128, 128], BF16))
    c.anti32 = e(nc.sbuf_tensor("c_anti32", [128, 128], F32))
    c.Ssel = e(nc.sbuf_tensor("c_Ssel", [128, CC, 128], BF16))
    c.Ssel32 = e(nc.sbuf_tensor("c_Ssel32", [128, CC, 128], F32))
    c.gamma = e(nc.sbuf_tensor("gammac", [128, DC], F32))
    c.beta = e(nc.sbuf_tensor("betac", [128, DC], F32))
    c.cstcol = e(nc.sbuf_tensor("cstcol", [128, 4], F32))
    for i, v in enumerate([EPS, -1.0, 1.0, 0.0]):
        try:
            nc.const_aps.register(F32, v, c.cstcol[:, i:i + 1])
        except Exception:
            pass

    @p.block.sync
    def _(sync):
        ins = []
        ins.append(sync.dma_start(out=c.ones_col[:], in_=io.ones_col))
        ins.append(sync.dma_start(out=c.ident32[:], in_=io.ident32))
        ins.append(sync.dma_start(out=c.anti32[:], in_=io.anti32))
        ins.append(sync.dma_start(out=c.Ssel32[:], in_=io.Ssel.rearrange("c p q -> p c q")))
        ins.append(sync.dma_start(out=c.gamma[:], in_=io.gamma.rearrange("(c p) one -> p (c one)", p=128), allow_slow_non_contiguous=True))
        ins.append(sync.dma_start(out=c.beta[:], in_=io.beta.rearrange("(c p) one -> p (c one)", p=128), allow_slow_non_contiguous=True))
        for i in ins:
            p.inc(i, "constld", 16)

    @p.block.vector
    def _(vector):
        p.wait(vector, "constld")
        vector.tensor_copy(c.identbf[:], c.ident32[:])
        vector.tensor_copy(c.antibf[:], c.anti32[:])
        vector.tensor_copy(c.Ssel[:], c.Ssel32[:])
        for i, v in enumerate([EPS, -1.0, 1.0, 0.0]):
            vector.memset(c.cstcol[:, i:i + 1], v)
    p.barrier()
    return c


def emit_ln(p, cm, scr, x_tiles, sq2, ps_stat, rows, rep, out_cb, which,
            nt=T, nchunks=DC):
    b = p.block
    nb = nt // 512

    @b.tensor
    def _(tensor):
        p.wait(tensor, which + "_in")
        for tb in range(nb):
            sl = slice(tb * 512, tb * 512 + 512)
            for kc in range(nchunks):
                ins = tensor.matmul(ps_stat[0:1, sl], cm.ones_col[:], x_tiles(kc)[:, sl],
                                    start=(kc == 0), stop=(kc == nchunks - 1))
        p.inc(ins, which + "_mmA")

    @b.scalar
    def _(scalar):
        p.wait(scalar, which + "_in")
        for kc in range(nchunks):
            if kc >= 2:
                p.wait(scalar, which + "_mmB", kc - 1)
            ins = scalar.activation(sq2[:, kc % 2, :nt], x_tiles(kc), ACT.Square)
            p.inc(ins, which + "_sq")

    @b.tensor
    def _(tensor):
        for kc in range(nchunks):
            p.wait(tensor, which + "_sq", kc + 1)
            for tb in range(nb):
                sl = slice(tb * 512, tb * 512 + 512)
                tensor.matmul(ps_stat[1:2, sl], cm.ones_col[:], sq2[:, kc % 2, sl],
                              start=(kc == 0), stop=(kc == nchunks - 1))
            p.inc(tensor.nop(), which + "_mmB")

    @b.vector
    def _(vector):
        p.wait(vector, which + "_mmA")
        p.wait(vector, which + "_mmB", nchunks)
        nd = 1.0 / (128 * nchunks)
        vector.tensor_scalar_mul(rows[0:1, :nt], ps_stat[0:1, :nt], nd)
        vector.tensor_scalar_mul(rows[1:2, :nt], ps_stat[1:2, :nt], nd)
        vector.tensor_mul(rows[2:3, :nt], rows[0:1, :nt], rows[0:1, :nt])
        ins = vector.tensor_sub(rows[1:2, :nt], rows[1:2, :nt], rows[2:3, :nt])
        p.inc(ins, which + "_var")

    @b.scalar
    def _(scalar):
        p.wait(scalar, which + "_var")
        ins = scalar.activation(rows[2:3, :nt], rows[1:2, :nt], ACT.Sqrt, bias=EPS)
        p.inc(ins, which + "_sqrt")

    @b.vector
    def _(vector):
        p.wait(vector, which + "_sqrt")
        ins = vector.reciprocal(rows[1:2, :nt], rows[2:3, :nt])
        p.inc(ins, which + "_rstd")

    @b.sync
    def _(sync):
        p.wait(sync, which + "_rstd")
        ins = sync.dma_start(out=scr.rowrep[0:2, 0:nt], in_=rows[0:2, :nt])
        p.inc(ins, which + "_rw", 16)
        p.wait(sync, which + "_rw")
        dr = scr.rowrep[0:2, 0:nt]
        src = bass.AP(dr.tensor, dr.offset, [[0, 128], [dr.ap[0][0], 2], [1, nt]])
        ins = sync.dma_start(out=rep[:, :, :nt], in_=src)
        p.inc(ins, which + "_rr", 16)

    @b.vector
    def _(vector):
        p.wait(vector, which + "_rr")
        for kc in range(nchunks):
            tmp = sq2[:, kc % 2, :nt]
            vector.tensor_sub(tmp, x_tiles(kc), rep[:, 0, :nt])
            vector.tensor_mul(tmp, tmp, rep[:, 1, :nt])
            out_cb(vector, kc, tmp)
        p.inc(vector.nop(), which + "_done")



SEG = 512
NSEG = T // SEG


def emit_mamba(p, io, scr, cm, tag, xn_tiles, out_dram):
    nc, b = p.nc, p.block
    w = lambda k: getattr(io, f"{tag}_{k}")
    GN = CC * NST
    mbstack = ExitStack()
    e = mbstack.enter_context

    mb = Bufs()
    mb.wtile = e(nc.sbuf_tensor(tag + "wtile", [128, 2, DC, 128], BF16))
    mb.upad = e(nc.sbuf_tensor(tag + "upad", [128, 2, 3 + T], BF16))
    mb.uact = e(nc.sbuf_tensor(tag + "uact", [128, 2, T], BF16))
    mb.diag4 = e(nc.sbuf_tensor(tag + "diag4", [128, 2, 4, 128], BF16))
    mb.diagD = e(nc.sbuf_tensor(tag + "diagDb", [128, 2, 128], BF16))
    mb.convw_cols = e(nc.sbuf_tensor(tag + "convw_cols", [128, CC, 4], F32))
    mb.convb_cols = e(nc.sbuf_tensor(tag + "convb_cols", [128, CC], F32))
    mb.D_cols = e(nc.sbuf_tensor(tag + "D_cols", [128, CC], F32))
    mb.dtb_cols = e(nc.sbuf_tensor(tag + "dtb_cols", [128, CC], F32))
    mb.A_cols = e(nc.sbuf_tensor(tag + "A_cols", [128, CC, NST], F32))
    mb.xprojw = e(nc.sbuf_tensor(tag + "xprojw", [128, CC, 96], BF16))
    mb.dtw = e(nc.sbuf_tensor(tag + "dtw", [64, CC, 128], BF16))
    mb.dtlin = e(nc.sbuf_tensor(tag + "dtlin", [64, T], BF16))
    mb.BCrow = e(nc.sbuf_tensor(tag + "BCrow", [2 * NST, T], BF16))
    mb.Cpk = e(nc.sbuf_tensor(tag + "Cpk", [128, T], BF16))
    mb.ps_io = e(nc.psum_tensor(tag + "ps_io", [128, T], F32))
    mb.ps_c = e(nc.psum_tensor(tag + "ps_c", [128, T], F32))

    @b.sync
    def _(sync):
        for ins in [
            sync.dma_start(out=mb.convw_cols[:], in_=w("convw").rearrange("(c p) k -> p c k", p=128), allow_slow_non_contiguous=True),
            sync.dma_start(out=mb.convb_cols[:], in_=w("convb").rearrange("(c p) one -> p (c one)", p=128), allow_slow_non_contiguous=True),
            sync.dma_start(out=mb.D_cols[:], in_=w("Dvec").rearrange("(c p) one -> p (c one)", p=128), allow_slow_non_contiguous=True),
            sync.dma_start(out=mb.dtb_cols[:], in_=w("dtb").rearrange("(c p) one -> p (c one)", p=128), allow_slow_non_contiguous=True),
            sync.dma_start(out=mb.A_cols[:], in_=w("Alog").rearrange("(c p) n -> p c n", p=128), allow_slow_non_contiguous=True),
        ]:
            p.inc(ins, "mb_cld", 16)

    @b.gpsimd
    def _(gpsimd):
        ins = gpsimd.dma_start(out=mb.xprojw[:], in_=w("xprojwt"))
        p.inc(ins, "mb_cldg", 16)
        ins = gpsimd.dma_start(out=mb.dtw[:], in_=w("dtwt"))
        p.inc(ins, "mb_cldg", 16)

    @b.scalar
    def _(scalar):
        p.wait(scalar, "mb_cld")
        p.wait(scalar, "mb_cldg")
        ins = scalar.activation(mb.A_cols[:], mb.A_cols[:], ACT.Exp)
        p.inc(ins, "mb_Aexp")

    @b.vector
    def _(vector):
        p.wait(vector, "mb_Aexp")
        vector.tensor_scalar_mul(mb.A_cols[:], mb.A_cols[:], -1.0)
        vector.memset(mb.upad[:, 0, 0:3], 0.0)
        ins = vector.memset(mb.upad[:, 1, 0:3], 0.0)
        p.inc(ins, "mb_prep")

    p.barrier()

    t_ = "M1"

    def wld_emit(eng, cc):
        return [eng.dma_start(out=mb.wtile[:, cc % 2], in_=w("inwTt")[cc])]

    def diag_emit(eng, cc):
        for k in range(4):
            ins = eng.tensor_scalar_mul(mb.diag4[:, cc % 2, k], cm.identbf[:],
                                        mb.convw_cols[:, cc, k:k + 1])
        return ins

    def umm_emit(eng, cc):
        for tb in range(NTB):
            sl = slice(tb * 512, (tb + 1) * 512)
            for kc in range(DC):
                eng.ldweights(mb.wtile[:, cc % 2, kc])
                ins = eng.matmul(mb.ps_io[:, sl], mb.wtile[:, cc % 2, kc],
                                 xn_tiles(kc)[:, sl], start=(kc == 0), stop=(kc == DC - 1))
        return ins

    def upev_emit(eng, cc):
        for tb in range(NTB):
            ins = eng.copy(mb.upad[:, cc % 2, 3 + tb * 512: 3 + (tb + 1) * 512],
                           mb.ps_io[:, tb * 512:(tb + 1) * 512])
        return ins

    def cmm_emit(eng, cc):
        for tb in range(NTB):
            sl = slice(tb * 512, (tb + 1) * 512)
            for k in range(4):
                eng.ldweights(mb.diag4[:, cc % 2, k])
                ins = eng.matmul(mb.ps_c[:, sl], mb.diag4[:, cc % 2, k],
                                 mb.upad[:, cc % 2, tb * 512 + k: tb * 512 + k + 512],
                                 start=(k == 0), stop=(k == 3))
        return ins

    def cev_emit(eng, cc):
        for tb in range(NTB):
            sl = slice(tb * 512, (tb + 1) * 512)
            ins = eng.activation(mb.uact[:, cc % 2, sl], mb.ps_c[:, sl], ACT.Silu,
                                 bias=mb.convb_cols[:, cc:cc + 1])
        return ins

    def uspill_emit(eng, cc):
        return [eng.dma_start(out=scr.u_act[cc * 128:(cc + 1) * 128, :], in_=mb.uact[:, cc % 2])]

    p.pipe(CC, [
        dict(name=t_ + "wld", eng="gpsimd", emit=wld_emit, dma=lambda i: 1,
             after=[(t_ + "umm", lambda i: i - 1)]),
        dict(name=t_ + "diag", eng="vector", emit=diag_emit,
             after=[(t_ + "cmm", lambda i: i - 1)]),
        dict(name=t_ + "umm", eng="tensor", emit=umm_emit,
             after=[(t_ + "wld", lambda i: i + 1), (t_ + "upev", lambda i: i)]),
        dict(name=t_ + "upev", eng="scalar", emit=upev_emit,
             after=[(t_ + "umm", lambda i: i + 1), (t_ + "cmm", lambda i: i - 1)]),
        dict(name=t_ + "cmm", eng="tensor", emit=cmm_emit,
             after=[(t_ + "upev", lambda i: i + 1), (t_ + "diag", lambda i: i + 1),
                    (t_ + "cev", lambda i: i)]),
        dict(name=t_ + "cev", eng="scalar", emit=cev_emit,
             after=[(t_ + "cmm", lambda i: i + 1), (t_ + "uspill", lambda i: i - 1)]),
        dict(name=t_ + "uspill", eng="gpsimd", emit=uspill_emit, dma=lambda i: 1,
             after=[(t_ + "cev", lambda i: i + 1)]),
    ])

    t_ = "Z"

    def zwld_emit(eng, cc):
        return [eng.dma_start(out=mb.wtile[:, cc % 2], in_=w("inwTt")[CC + cc])]

    def zmm_emit(eng, cc):
        for tb in range(NTB):
            sl = slice(tb * 512, (tb + 1) * 512)
            for kc in range(DC):
                eng.ldweights(mb.wtile[:, cc % 2, kc])
                ins = eng.matmul(mb.ps_io[:, sl], mb.wtile[:, cc % 2, kc],
                                 xn_tiles(kc)[:, sl], start=(kc == 0), stop=(kc == DC - 1))
        return ins

    def zev_emit(eng, cc):
        for tb in range(NTB):
            sl = slice(tb * 512, (tb + 1) * 512)
            ins = eng.activation(mb.uact[:, cc % 2, sl], mb.ps_io[:, sl], ACT.Silu)
        return ins

    def zspill_emit(eng, cc):
        return [eng.dma_start(out=scr.silu_z[cc * 128:(cc + 1) * 128, :], in_=mb.uact[:, cc % 2])]

    p.pipe(CC, [
        dict(name=t_ + "wld", eng="gpsimd", emit=zwld_emit, dma=lambda i: 1,
             after=[(t_ + "mm", lambda i: i - 1)]),
        dict(name=t_ + "mm", eng="tensor", emit=zmm_emit,
             after=[(t_ + "wld", lambda i: i + 1), (t_ + "ev", lambda i: i)]),
        dict(name=t_ + "ev", eng="scalar", emit=zev_emit,
             after=[(t_ + "mm", lambda i: i + 1), (t_ + "spill", lambda i: i - 1)]),
        dict(name=t_ + "spill", eng="gpsimd", emit=zspill_emit, dma=lambda i: 1,
             after=[(t_ + "ev", lambda i: i + 1)]),
    ])

    p.barrier()

    t_ = "X"

    def xuld_emit(eng, cc):
        return [eng.dma_start(out=mb.uact[:, cc % 2], in_=scr.u_act[cc * 128:(cc + 1) * 128, :])]

    def xp_emit(eng, cc):
        for tb in range(NTB):
            sl = slice(tb * 512, (tb + 1) * 512)
            eng.ldweights(mb.xprojw[:, cc])
            ins = eng.matmul(mb.ps_io[0:96, sl], mb.xprojw[:, cc], mb.uact[:, cc % 2, sl],
                             start=(cc == 0), stop=(cc == CC - 1))
        return ins

    p.pipe(CC, [
        dict(name=t_ + "uld", eng="sync", emit=xuld_emit, dma=lambda i: 1,
             after=[(t_ + "xp", lambda i: i - 1)]),
        dict(name=t_ + "xp", eng="tensor", emit=xp_emit,
             after=[(t_ + "uld", lambda i: i + 1)]),
    ])

    @b.scalar
    def _(scalar):
        p.wait(scalar, "Xxp")
        ins = scalar.copy(mb.dtlin[:], mb.ps_io[0:64, :])
        p.inc(ins, "Xxdev")
        ins = scalar.copy(mb.BCrow[:], mb.ps_io[64:96, :])
        p.inc(ins, "Xxdev")
        p.inc(scalar.nop(), "Xxdev")

    @b.sync
    def _(sync):
        p.wait(sync, "Xxdev")
        ins = sync.dma_start(out=scr.bcrep[0:NST, :], in_=mb.BCrow[0:NST, :])
        p.inc(ins, "Xbcw", 16)
        ins = sync.dma_start(out=scr.bcrep[NST:2 * NST, :], in_=mb.BCrow[NST:2 * NST, :])
        p.inc(ins, "Xbcw", 16)
        p.wait(sync, "Xbcw")
        dr2 = scr.bcrep[NST:2 * NST, :]
        src2 = bass.AP(dr2.tensor, dr2.offset, [[0, 8], [dr2.ap[0][0], NST], [1, T]])
        ins = sync.dma_start(out=mb.Cpk[:], in_=src2)
        p.inc(ins, "Xbcr", 16)
        p.wait(sync, "Xbcr")

    p.barrier()

    with ExitStack() as m3ctx:
        e3 = m3ctx.enter_context
        m3 = Bufs()
        m3.Brep = e3(nc.sbuf_tensor(tag + "Brep", [128, NST, SEG], BF16))
        m3.uacth = e3(nc.sbuf_tensor(tag + "uacth", [128, 2, SEG], BF16))
        m3.szh = e3(nc.sbuf_tensor(tag + "szh", [128, 2, SEG], BF16))
        m3.dt_s = e3(nc.sbuf_tensor(tag + "dt_s", [128, SEG], F32))
        m3.sp1 = e3(nc.sbuf_tensor(tag + "sp1", [128, SEG], F32))
        m3.sp2 = e3(nc.sbuf_tensor(tag + "sp2", [128, SEG], F32))
        m3.dA2 = e3(nc.sbuf_tensor(tag + "dA2", [128, 2, SEG], F32))
        m3.dBu2 = e3(nc.sbuf_tensor(tag + "dBu2", [128, 2, SEG], F32))
        m3.du2 = e3(nc.sbuf_tensor(tag + "du2", [128, 2, SEG], F32))
        m3.H = e3(nc.sbuf_tensor(tag + "H", [128, 2, NST, SEG], BF16))
        m3.packed = e3(nc.sbuf_tensor(tag + "packed", [128, 2, SEG], BF16))
        m3.inicol = e3(nc.sbuf_tensor(tag + "inicol", [128, CC, NST], F32))
        m3.ybuf = e3(nc.sbuf_tensor(tag + "ybuf", [128, 2, SEG], BF16))

        NAst = NSEG * GN
        NBst = NSEG * CC * CC

        def scn(s):
            seg, q = divmod(s, GN)
            cc, n = divmod(q, NST)
            return seg, cc, n

        def rcj(r):
            seg, q = divmod(r, CC * CC)
            cc, j = divmod(q, CC)
            return seg, cc, j

        def brep_emit(eng, s):
            seg, cc, n = scn(s)
            if cc == 0 and n == 0:
                dr = scr.bcrep[0:NST, seg * SEG:(seg + 1) * SEG]
                src = bass.AP(dr.tensor, dr.offset, [[0, 128], [dr.ap[0][0], NST], [1, SEG]])
                return [eng.dma_start(out=m3.Brep[:], in_=src)]
            return []

        def uld_emit(eng, s):
            seg, cc, n = scn(s)
            if n == 0:
                par = (s // NST) % 2
                rs = slice(cc * 128, (cc + 1) * 128)
                hs = slice(seg * SEG, (seg + 1) * SEG)
                return [eng.dma_start(out=m3.uacth[:, par], in_=scr.u_act[rs, hs]),
                        eng.dma_start(out=m3.szh[:, par], in_=scr.silu_z[rs, hs])]
            return []

        def dtq_emit(eng, s):
            seg, cc, n = scn(s)
            if n == 0:
                eng.ldweights(mb.dtw[:, cc])
                return eng.matmul(mb.ps_io[:, 0:SEG], mb.dtw[:, cc],
                                  mb.dtlin[:, seg * SEG:(seg + 1) * SEG],
                                  start=True, stop=True)
            return None

        def dtev_emit(eng, s):
            seg, cc, n = scn(s)
            if n == 0:
                bias = mb.dtb_cols[:, cc:cc + 1]
                eng.activation(m3.sp1[:], mb.ps_io[:, 0:SEG], ACT.Abs, bias=bias)
                eng.activation(m3.sp2[:], m3.sp1[:], ACT.Exp, scale=-1.0)
                eng.activation(m3.sp1[:], m3.sp2[:], ACT.Ln, bias=1.0)
                return eng.activation(m3.sp2[:], mb.ps_io[:, 0:SEG], ACT.Relu, bias=bias)
            return None

        def dtv_emit(eng, s):
            seg, cc, n = scn(s)
            if n == 0:
                return eng.tensor_add(m3.dt_s[:], m3.sp1[:], m3.sp2[:])
            return None

        def diagD_emit(eng, s):
            seg, cc, n = scn(s)
            if n == 0:
                return eng.tensor_scalar_mul(mb.diagD[:, (s // NST) % 2], cm.identbf[:],
                                             mb.D_cols[:, cc:cc + 1])
            return None

        def duv_emit(eng, s):
            seg, cc, n = scn(s)
            if n == 0:
                par = (s // NST) % 2
                return eng.tensor_mul(m3.du2[:, par], m3.dt_s[:], m3.uacth[:, par])
            return None

        def dA_emit(eng, s):
            seg, cc, n = scn(s)
            return eng.activation(m3.dA2[:, s % 2], m3.dt_s[:], ACT.Exp,
                                  scale=mb.A_cols[:, cc, n:n + 1])

        def dbu_emit(eng, s):
            seg, cc, n = scn(s)
            par = (s // NST) % 2
            return eng.tensor_mul(m3.dBu2[:, s % 2], m3.du2[:, par], m3.Brep[:, n])

        def scan_emit(eng, s):
            seg, cc, n = scn(s)
            par = (s // NST) % 2
            init = 0.0 if seg == 0 else m3.inicol[:, cc, n:n + 1]
            return eng.tensor_tensor_scan(m3.H[:, par, n], m3.dA2[:, s % 2],
                                          m3.dBu2[:, s % 2], init, ALU.mult, ALU.add)

        def lcol_emit(eng, s):
            seg, cc, n = scn(s)
            if seg < NSEG - 1:
                par = (s // NST) % 2
                return eng.copy(m3.inicol[:, cc, n:n + 1], m3.H[:, par, n, SEG - 1:SEG])
            return None

        def pack_emit(eng, r):
            seg, cc, j = rcj(r)
            par = (r // CC) % 2
            src = m3.H[8 * j: 8 * (j + 1), par]
            return [eng.dma_start(out=m3.packed[:, r % 2], in_=src)]

        def cmul_emit(eng, r):
            seg, cc, j = rcj(r)
            hs = slice(seg * SEG, (seg + 1) * SEG)
            return eng.tensor_mul(m3.packed[:, r % 2], m3.packed[:, r % 2], mb.Cpk[:, hs])

        def nrmm_emit(eng, r):
            seg, cc, j = rcj(r)
            par = (r // CC) % 2
            eng.ldweights(cm.Ssel[:, j])
            return eng.matmul(mb.ps_c[:, par * SEG:(par + 1) * SEG], cm.Ssel[:, j],
                              m3.packed[:, r % 2], start=(j == 0), stop=False)

        def yuD_emit(eng, r):
            seg, cc, j = rcj(r)
            if j != CC - 1:
                return None
            par = (r // CC) % 2
            eng.ldweights(mb.diagD[:, par])
            return eng.matmul(mb.ps_c[:, par * SEG:(par + 1) * SEG], mb.diagD[:, par],
                              m3.uacth[:, par], start=False, stop=True)

        def yev_emit(eng, r):
            seg, cc, j = rcj(r)
            if j != CC - 1:
                return None
            par = (r // CC) % 2
            return eng.tensor_mul(m3.ybuf[:, par], mb.ps_c[:, par * SEG:(par + 1) * SEG],
                                  m3.szh[:, par])

        def yspill_emit(eng, r):
            seg, cc, j = rcj(r)
            if j != CC - 1:
                return []
            par = (r // CC) % 2
            hs = slice(seg * SEG, (seg + 1) * SEG)
            return [eng.dma_start(out=scr.y_tot[cc * 128:(cc + 1) * 128, hs],
                                  in_=m3.ybuf[:, par])]

        def bshift(emitfn):
            def f(eng, s):
                r = s - NST
                if r < 0 or r >= NBst:
                    return [] if emitfn in (pack_emit, yspill_emit) else None
                return emitfn(eng, r)
            return f

        NTOT = NAst + NST

        def ashift(emitfn, retlist=False):
            def f(eng, s):
                if s >= NAst:
                    return [] if retlist else None
                return emitfn(eng, s)
            return f

        gA = lambda s: min(s, NAst - 1) // NST
        p.pipe(NTOT, [
            dict(name="Sbrep", eng="sync", emit=ashift(brep_emit, True),
                 dma=lambda s: 1 if (s < NAst and s % GN == 0) else 0,
                 after=[("Sdbu", lambda s: (s // GN) * GN)]),
            dict(name="Suld", eng="sync", emit=ashift(uld_emit, True),
                 dma=lambda s: 2 if (s < NAst and s % NST == 0) else 0,
                 after=[("Syev", lambda s: s), ("SyuD", lambda s: s)]),
            dict(name="Sdtq", eng="tensor", emit=ashift(dtq_emit),
                 after=[("Sdtev", lambda s: s - NST + 1)]),
            dict(name="Sdtev", eng="scalar", emit=ashift(dtev_emit),
                 after=[("Sdtq", lambda s: s + 1), ("Sdtv", lambda s: s - NST + 1)]),
            dict(name="Sdtv", eng="vector", emit=ashift(dtv_emit),
                 after=[("Sdtev", lambda s: s + 1), ("SdA", lambda s: s),
                        ("Sduv", lambda s: s - NST + 1)]),
            dict(name="SdiagD", eng="vector", emit=ashift(diagD_emit),
                 after=[("SyuD", lambda s: s)]),
            dict(name="Sduv", eng="vector", emit=ashift(duv_emit),
                 after=[("Suld", lambda s: (s // NST) * NST + 1),
                        ("Sdtv", lambda s: s + 1),
                        ("Sdbu", lambda s: s - NST)]),
            dict(name="SdA", eng="scalar", emit=ashift(dA_emit),
                 after=[("Sdtv", lambda s: s - (s % NST) + 1),
                        ("Sscan", lambda s: s - 1)]),
            dict(name="Slcol", eng="scalar", emit=ashift(lcol_emit),
                 after=[("Sscan", lambda s: s + 1)]),
            dict(name="Sdbu", eng="vector", emit=ashift(dbu_emit),
                 after=[("Sduv", lambda s: s - (s % NST) + 1),
                        ("Sbrep", lambda s: (s // GN) * GN + 1),
                        ("Sscan", lambda s: s - 1)]),
            dict(name="Sscan", eng="vector", emit=ashift(scan_emit),
                 after=[("SdA", lambda s: s + 1), ("Sdbu", lambda s: s + 1),
                        ("Slcol", lambda s: s - NST),
                        ("Spack", lambda s: NST * (s // NST))]),
            dict(name="Spack", eng="sync", emit=bshift(pack_emit),
                 dma=lambda s: 1 if 0 <= s - NST < NBst else 0,
                 after=[("Sscan", lambda s: NST * (s // NST)),
                        ("Snrmm", lambda s: s - 1)]),
            dict(name="Scmul", eng="vector", emit=bshift(cmul_emit),
                 after=[("Spack", lambda s: s + 1)]),
            dict(name="Snrmm", eng="tensor", emit=bshift(nrmm_emit),
                 after=[("Scmul", lambda s: s + 1),
                        ("Syev", lambda s: s - (s % NST) - NST)]),
            dict(name="SyuD", eng="tensor", emit=bshift(yuD_emit),
                 after=[("SdiagD", lambda s: s - NST - (CC - 1) + 1)]),
            dict(name="Syev", eng="vector", emit=bshift(yev_emit),
                 after=[("SyuD", lambda s: s + 1),
                        ("Syspill", lambda s: NST * (s // NST - 1))]),
            dict(name="Syspill", eng="gpsimd", emit=bshift(yspill_emit),
                 dma=lambda s: 1 if (0 <= s - NST < NBst and (s - NST) % CC == CC - 1) else 0,
                 after=[("Syev", lambda s: s + 1)]),
        ])

    p.barrier()

    t_ = "O"
    with ExitStack() as m4ctx:
        e4 = m4ctx.enter_context
        m4 = Bufs()
        m4.owt = e4(nc.sbuf_tensor(tag + "owt", [128, CC, DC, 128], BF16))
        m4.ysl = e4(nc.sbuf_tensor(tag + "ysl", [128, 2, CC, 512], BF16))
        m4.obuf = e4(nc.sbuf_tensor(tag + "obuf", [128, 2, 512], F32))
        psb = lambda dk: (mb.ps_io if dk < 4 else mb.ps_c)[:, (dk % 4) * 512:(dk % 4 + 1) * 512]

        p.sem("Ooevd"); p.sem("Oospld@0"); p.sem("Oospld@1")
        b_oevd = p.expect("Ooevd")
        b_osp = [p.expect("Oospld@0"), p.expect("Oospld@1")]

        @b.gpsimd
        def _(gpsimd):
            ins = gpsimd.dma_start(out=m4.owt[:], in_=w("outwTt"))
            p.inc(ins, "Owld", 16)

        def yld_emit(eng, tb):
            sl = slice(tb * 512, (tb + 1) * 512)
            return [eng.dma_start(out=m4.ysl[:, tb % 2],
                                  in_=scr.y_tot[:, sl].rearrange("(c p) t -> p c t", p=128))]

        def omm_emit(eng, tb):
            for dk in range(DC):
                for cc in range(CC):
                    eng.ldweights(m4.owt[:, cc, dk])
                    ins = eng.matmul(psb(dk), m4.owt[:, cc, dk], m4.ysl[:, tb % 2, cc],
                                     start=(cc == 0), stop=(cc == CC - 1))
            return ins

        def oevspl_sc(eng, tb):
            for dk in range(DC):
                idx = tb * DC + dk
                v = 16 * (idx // 2)
                if v > 0:
                    eng.wait_ge(p.sem("Oospld@" + str(idx % 2))[0], b_osp[idx % 2] + v)
                ins = eng.copy(m4.obuf[:, dk % 2], psb(dk))
                p.inc(ins, "Ooevd", 1)
            return None

        def oevspl_gp(eng, tb):
            for dk in range(DC):
                idx = tb * DC + dk
                eng.wait_ge(p.sem("Ooevd")[0], b_oevd + idx + 1)
                ins = eng.dma_start(out=out_dram[dk * 128:(dk + 1) * 128,
                                                 tb * 512:(tb + 1) * 512],
                                    in_=m4.obuf[:, dk % 2])
                p.inc(ins, "Oospld@" + str(idx % 2), 16)
            return None
        p.pipe(NTB, [
            dict(name="Oyld", eng="sync", emit=yld_emit, dma=lambda i: 1,
                 after=[("Oomm", lambda i: i - 1)]),
            dict(name="Oomm", eng="tensor", emit=omm_emit,
                 after=[("Oyld", lambda i: i + 1),
                        ("Owld", lambda i: 0),
                        ("Ooev", lambda i: i)]),
            dict(name="Ooev", eng="scalar", emit=oevspl_sc,
                 after=[("Oomm", lambda i: i + 1)]),
            dict(name="Oospl", eng="gpsimd", emit=oevspl_gp, after=[]),
        ])

    p.barrier()
    mbstack.close()



def emit_ln_phase(p, io, scr, cm, lnb, x_src_dram, which_unused, out_mode):
    nc, b = p.nc, p.block
    t_ = "LN"

    def xld_emit(eng, kc):
        return [eng.dma_start(out=lnb.xbuf[:, kc % 2], in_=x_src_dram[kc * 128:(kc + 1) * 128, :])]

    def sqa_emit(eng, kc):
        return eng.activation(lnb.sq2[:, kc % 2], lnb.xbuf[:, kc % 2], ACT.Square)

    def mma_emit(eng, kc):
        for tb in range(NTB):
            sl = slice(tb * 512, (tb + 1) * 512)
            tensorins = eng.matmul(lnb.ps_stat[0:1, 0, sl], cm.ones_col[:], lnb.xbuf[:, kc % 2, sl],
                                   start=(kc == 0), stop=(kc == DC - 1))
            tensorins = eng.matmul(lnb.ps_stat[0:1, 1, sl], cm.ones_col[:], lnb.sq2[:, kc % 2, sl],
                                   start=(kc == 0), stop=(kc == DC - 1))
        return tensorins

    p.pipe(DC, [
        dict(name=t_ + "xld", eng="sync", emit=xld_emit, dma=lambda i: 1,
             after=[(t_ + "mma", lambda i: i - 1)]),
        dict(name=t_ + "sqa", eng="scalar", emit=sqa_emit,
             after=[(t_ + "xld", lambda i: i + 1), (t_ + "mma", lambda i: i - 1)]),
        dict(name=t_ + "mma", eng="tensor", emit=mma_emit,
             after=[(t_ + "sqa", lambda i: i + 1)]),
    ])

    def ln_finish(suffix, wait_sem):
        @b.vector
        def _(vector):
            p.wait(vector, wait_sem)
            nd = 1.0 / D
            vector.tensor_scalar_mul(lnb.rows[0:1, 0], lnb.ps_stat[0:1, 0], nd)
            vector.tensor_scalar_mul(lnb.rows[0:1, 1], lnb.ps_stat[0:1, 1], nd)
            vector.tensor_mul(lnb.rows[0:1, 2], lnb.rows[0:1, 0], lnb.rows[0:1, 0])
            ins = vector.tensor_sub(lnb.rows[0:1, 1], lnb.rows[0:1, 1], lnb.rows[0:1, 2])
            p.inc(ins, t_ + suffix + "var")

        @b.scalar
        def _(scalar):
            p.wait(scalar, t_ + suffix + "var")
            ins = scalar.activation(lnb.rows[0:1, 2], lnb.rows[0:1, 1], ACT.Sqrt, bias=EPS)
            p.inc(ins, t_ + suffix + "sqrt")

        @b.vector
        def _(vector):
            p.wait(vector, t_ + suffix + "sqrt")
            ins = vector.reciprocal(lnb.rows[0:1, 1], lnb.rows[0:1, 2])
            p.inc(ins, t_ + suffix + "rstd")

        @b.sync
        def _(sync):
            p.wait(sync, t_ + suffix + "rstd")
            ins = sync.dma_start(out=scr.rowrep[0:2, :], in_=lnb.rows[0:1, 0:2])
            p.inc(ins, t_ + suffix + "rw", 16)
            p.wait(sync, t_ + suffix + "rw")
            dr = scr.rowrep[0:2, :]
            src = bass.AP(dr.tensor, dr.offset, [[0, 128], [dr.ap[0][0], 2], [1, T]])
            ins = sync.dma_start(out=lnb.rep[:], in_=src)
            p.inc(ins, t_ + suffix + "rr", 16)
            p.wait(sync, t_ + suffix + "rr")
            p.inc(sync.nop(), t_ + suffix + "repdone")

    ln_finish("A", t_ + "mma")

    if out_mode == "plain":
        def xld2_emit(eng, kc):
            return [eng.dma_start(out=lnb.xbuf[:, kc % 2], in_=x_src_dram[kc * 128:(kc + 1) * 128, :])]

        def app_emit(eng, kc):
            tmp = lnb.sq2[:, kc % 2]
            eng.tensor_sub(tmp, lnb.xbuf[:, kc % 2], lnb.rep[:, 0])
            return eng.tensor_mul(lnb.xn[:, kc], tmp, lnb.rep[:, 1])

        p.pipe(DC, [
            dict(name=t_ + "xld2", eng="sync", emit=xld2_emit, dma=lambda i: 1,
                 after=[(t_ + "app", lambda i: i - 1),
                        (t_ + "Arepdone", lambda i: 0)]),
            dict(name=t_ + "app", eng="vector", emit=app_emit,
                 after=[(t_ + "xld2", lambda i: i + 1)]),
        ])
        p.barrier()
        return

    def xld2_emit(eng, kc):
        return [eng.dma_start(out=lnb.xbuf[:, kc % 2], in_=x_src_dram[kc * 128:(kc + 1) * 128, :])]

    def app1_emit(eng, kc):
        ln1k = lnb.ln1[:, kc]
        eng.tensor_sub(ln1k, lnb.xbuf[:, kc % 2], lnb.rep[:, 0])
        eng.tensor_mul(ln1k, ln1k, lnb.rep[:, 1])
        return eng.tensor_add(lnb.s1b[:, kc % 2], lnb.xbuf[:, kc % 2], ln1k)

    def s1sp_emit(eng, kc):
        return [eng.dma_start(out=scr.s1[kc * 128:(kc + 1) * 128, :], in_=lnb.s1b[:, kc % 2])]

    def sqb_emit(eng, kc):
        return eng.activation(lnb.sq2[:, kc % 2], lnb.ln1[:, kc], ACT.Square)

    def mmb_emit(eng, kc):
        for tb in range(NTB):
            sl = slice(tb * 512, (tb + 1) * 512)
            ins = eng.matmul(lnb.ps_stat[0:1, 0, sl], cm.ones_col[:], lnb.ln1[:, kc, sl],
                             start=(kc == 0), stop=(kc == DC - 1))
            ins = eng.matmul(lnb.ps_stat[0:1, 1, sl], cm.ones_col[:], lnb.sq2[:, kc % 2, sl],
                             start=(kc == 0), stop=(kc == DC - 1))
        return ins

    p.pipe(DC, [
        dict(name=t_ + "xld2", eng="sync", emit=xld2_emit, dma=lambda i: 1,
             after=[(t_ + "app1", lambda i: i - 1),
                    (t_ + "Arepdone", lambda i: 0)]),
        dict(name=t_ + "app1", eng="vector", emit=app1_emit,
             after=[(t_ + "xld2", lambda i: i + 1),
                    (t_ + "s1sp", lambda i: i - 1)]),
        dict(name=t_ + "s1sp", eng="gpsimd", emit=s1sp_emit, dma=lambda i: 1,
             after=[(t_ + "app1", lambda i: i + 1)]),
        dict(name=t_ + "sqb", eng="scalar", emit=sqb_emit,
             after=[(t_ + "app1", lambda i: i + 1), (t_ + "mmb", lambda i: i - 1)]),
        dict(name=t_ + "mmb", eng="tensor", emit=mmb_emit,
             after=[(t_ + "sqb", lambda i: i + 1), (t_ + "Avar", lambda i: 0)]),
    ])

    ln_finish("B", t_ + "mmb")

    @b.vector
    def _(vector):
        p.wait(vector, t_ + "Brepdone")
        for kc in range(DC):
            tmp = lnb.sq2[:, kc % 2]
            vector.tensor_sub(tmp, lnb.ln1[:, kc], lnb.rep[:, 0])
            vector.tensor_mul(tmp, tmp, lnb.rep[:, 1])
            ins = vector.tensor_scalar(lnb.xn[:, kc], tmp, cm.gamma[:, kc:kc + 1],
                                       cm.beta[:, kc:kc + 1], ALU.mult, ALU.add)
        p.inc(ins, t_ + "xn_done")
    p.barrier()



def emit_reverse_plane(p, cm, src_dram, dst_dram, nrows, which_unused, fbuf):
    nc, b = p.nc, p.block
    t_ = "RV"
    NC_ = nrows // 128
    NT_ = T // 128
    ident = cm.ident32 if fbuf.ch.dtype == F32 else cm.identbf
    anti = cm.anti32 if fbuf.ch.dtype == F32 else cm.antibf

    def ld_emit(eng, kc):
        return [eng.dma_start(out=fbuf.ch[:, kc % 2], in_=src_dram[kc * 128:(kc + 1) * 128, :])]

    def trpair_tensor(eng, kc):
        for tt in range(NT_):
            idx = kc * NT_ + tt
            if idx > 0:
                p.wait(eng, t_ + "ev1", rvb["ev1"] + idx)
            src_bl = fbuf.ch[:, kc % 2, (NT_ - 1 - tt) * 128:(NT_ - tt) * 128]
            ins = eng.transpose(fbuf.ps[:, 0, 0:128], src_bl, ident[:])
            p.inc(ins, t_ + "tr1", 1)
            p.wait(eng, t_ + "ev1", rvb["ev1"] + idx + 1)
            if idx > 0:
                p.wait(eng, t_ + "ev2", rvb["ev2"] + idx)
            ins = eng.transpose(fbuf.ps[:, 1, 0:128], fbuf.tmp[:], anti[:])
            p.inc(ins, t_ + "tr2", 1)
        return None

    def ev_scalar(eng, kc):
        for tt in range(NT_):
            idx = kc * NT_ + tt
            p.wait(eng, t_ + "tr1", rvb["tr1"] + idx + 1)
            ins = eng.copy(fbuf.tmp[:], fbuf.ps[:, 0, 0:128])
            p.inc(ins, t_ + "ev1", 1)
            p.wait(eng, t_ + "tr2", rvb["tr2"] + idx + 1)
            ins = eng.copy(fbuf.och[:, kc % 2, tt * 128:(tt + 1) * 128], fbuf.ps[:, 1, 0:128])
            p.inc(ins, t_ + "ev2", 1)
        return None

    def st_emit(eng, kc):
        return [eng.dma_start(out=dst_dram[kc * 128:(kc + 1) * 128, :], in_=fbuf.och[:, kc % 2])]

    for nm in ("tr1", "tr2", "ev1", "ev2"):
        p.sem(t_ + nm)
    rvb = {nm: p.expect(t_ + nm) for nm in ("tr1", "tr2", "ev1", "ev2")}
    p.pipe(NC_, [
        dict(name=t_ + "ld", eng="sync", emit=ld_emit, dma=lambda i: 1,
             after=[(t_ + "trp", lambda i: i - 1)]),
        dict(name=t_ + "trp", eng="tensor", emit=trpair_tensor,
             after=[(t_ + "ld", lambda i: i + 1)]),
        dict(name=t_ + "ev", eng="scalar", emit=ev_scalar,
             after=[(t_ + "st", lambda i: i - 1)]),
        dict(name=t_ + "st", eng="gpsimd", emit=st_emit, dma=lambda i: 1,
             after=[(t_ + "ev", lambda i: i + 1)]),
    ])
    p.barrier()



def emit_attn_a(p, io, scr, cm, x_src_dram, x_out_dram, dbg=None):
    nc, b = p.nc, p.block
    astack = ExitStack()
    e = astack.enter_context
    xn = e(nc.sbuf_tensor("xnA", [128, DC, T], BF16))
    xn_tiles = lambda kc: xn[:, kc]

    with ExitStack() as c1:
        e1 = c1.enter_context
        lnb = Bufs()
        lnb.xn = xn
        lnb.xbuf = e1(nc.sbuf_tensor("ln_xbuf", [128, 2, T], F32))
        lnb.sq2 = e1(nc.sbuf_tensor("ln_sq2", [128, 2, T], F32))
        lnb.rep = e1(nc.sbuf_tensor("ln_rep", [128, 2, T], F32))
        lnb.rows = e1(nc.sbuf_tensor("ln_rows", [1, 4, T], F32))
        lnb.s1b = e1(nc.sbuf_tensor("ln_s1b", [128, 2, T], F32))
        lnb.ln1 = e1(nc.sbuf_tensor("ln_ln1", [128, DC, T], F32))
        lnb.ps_stat = e1(nc.psum_tensor("ln_ps_stat", [1, 2, T], F32))
        emit_ln_phase(p, io, scr, cm, lnb, x_src_dram, "LA", "attn_a")

    @b.sync
    def _(sync):
        ins = sync.dma_start(out=scr.xn_d.rearrange("(c p) t -> p c t", p=128), in_=xn[:])
        p.inc(ins, "xnsp", 16)
        p.wait(sync, "xnsp")
    p.barrier()

    with ExitStack() as c2:
        e2 = c2.enter_context
        fb = Bufs()
        fb.ch = e2(nc.sbuf_tensor("rv_ch", [128, 2, T], BF16))
        fb.tmp = e2(nc.sbuf_tensor("rv_tmp", [128, 128], BF16))
        fb.och = e2(nc.sbuf_tensor("rv_och", [128, 2, T], BF16))
        fb.ps = e2(nc.psum_tensor("rv_ps", [128, 2, 1024], BF16))
        emit_reverse_plane(p, cm, scr.xn_d, scr.xn_rev, D, "RXN", fb)

    emit_mamba(p, io, scr, cm, "f", xn_tiles, scr.outdir)

    @b.sync
    def _(sync):
        ins = sync.dma_start(out=xn[:], in_=scr.xn_rev.rearrange("(c p) t -> p c t", p=128))
        p.inc(ins, "xnrld", 16)
        p.wait(sync, "xnrld")
    p.barrier()

    emit_mamba(p, io, scr, cm, "b", xn_tiles, scr.outdir2)

    with ExitStack() as c3:
        e3 = c3.enter_context
        fb = Bufs()
        fb.ch = e3(nc.sbuf_tensor("rvo_ch", [128, 2, T], F32))
        fb.tmp = e3(nc.sbuf_tensor("rvo_tmp", [128, 128], F32))
        fb.och = e3(nc.sbuf_tensor("rvo_och", [128, 2, T], F32))
        fb.ps = e3(nc.psum_tensor("rvo_ps", [128, 2, 512], F32))
        emit_reverse_plane(p, cm, scr.outdir2, scr.outdir2r, D, "RVO", fb)

    with ExitStack() as c4:
        e4 = c4.enter_context
        cb = Bufs()
        cb.a = e4(nc.sbuf_tensor("cb_a", [128, 2, T], F32))
        cb.bb = e4(nc.sbuf_tensor("cb_b", [128, 2, T], F32))
        cb.c = e4(nc.sbuf_tensor("cb_c", [128, 2, T], F32))

        def cld_emit(eng, kc):
            rs = slice(kc * 128, (kc + 1) * 128)
            return [eng.dma_start(out=cb.a[:, kc % 2], in_=scr.s1[rs, :]),
                    eng.dma_start(out=cb.bb[:, kc % 2], in_=scr.outdir[rs, :]),
                    eng.dma_start(out=cb.c[:, kc % 2], in_=scr.outdir2r[rs, :])]

        def cadd_emit(eng, kc):
            eng.tensor_add(cb.a[:, kc % 2], cb.a[:, kc % 2], cb.bb[:, kc % 2])
            return eng.tensor_add(cb.a[:, kc % 2], cb.a[:, kc % 2], cb.c[:, kc % 2])

        def cst_emit(eng, kc):
            return [eng.dma_start(out=x_out_dram[kc * 128:(kc + 1) * 128, :],
                                  in_=cb.a[:, kc % 2])]

        p.pipe(DC, [
            dict(name="CBld", eng="sync", emit=cld_emit, dma=lambda i: 3,
                 after=[("CBadd", lambda i: i - 1), ("CBst", lambda i: i - 1)]),
            dict(name="CBadd", eng="vector", emit=cadd_emit,
                 after=[("CBld", lambda i: i + 1), ("CBst", lambda i: i - 1)]),
            dict(name="CBst", eng="gpsimd", emit=cst_emit, dma=lambda i: 1,
                 after=[("CBadd", lambda i: i + 1)]),
        ])
    p.barrier()
    astack.close()


def emit_dram_dump(p, srcs):
    b = p.block

    @b.gpsimd
    def _(gpsimd):
        for dst, src in srcs:
            ins = gpsimd.dma_start(out=dst, in_=src)
            p.inc(ins, "dbgdump", 16)
        p.wait(gpsimd, "dbgdump")
    p.barrier()


def finalize(p, io, scr, final_src):
    b = p.block

    @b.sync
    def _(sync):
        ins = sync.dma_start(out=io.out, in_=final_src)
        p.inc(ins, "finout", 16)
        p.wait(sync, "finout")


def build_upto(upto="attn_a", dbg_names=()):
    ctx = ExitStack()
    nc = bass.Bass("TRN2", target_bir_lowering=False, detect_race_conditions=False)
    io = declare_io(nc, dbg_names)
    scr = declare_scratch(nc)
    upto_levels = {"attn_a": 0, "ffn1": 1, "attn_b": 2, "ffn2": 3, "full": 3}
    assert upto in upto_levels
    with ctx:
        block = ctx.enter_context(nc.Block())
        p = Prog(nc, block, ctx)
        cm = alloc_common(p, io)
        emit_attn_a(p, io, scr, cm, io.x_fm, scr.xcur)
        if upto in ("ffn1", "attn_b", "ffn2", "full"):
            p.recycle(keep=("BAR", "constld"))
            emit_ffn(p, io, scr, cm, scr.xcur, scr.xcur2, "1")
        if upto in ("attn_b", "ffn2", "full"):
            p.recycle(keep=("BAR", "constld"))
            emit_attn_b(p, io, scr, cm, scr.xcur2, scr.xcur)
        if upto in ("ffn2", "full"):
            p.recycle(keep=("BAR", "constld"))
            emit_ffn(p, io, scr, cm, scr.xcur, scr.xcur2, "2")
        if dbg_names:
            pairs = []
            for name, shp, dt in dbg_names:
                src = {
                    "dbg_xn": scr.xn_d, "dbg_uact": scr.u_act, "dbg_sz": scr.silu_z,
                    "dbg_ytot": scr.y_tot, "dbg_outf": scr.outdir,
                    "dbg_outb": scr.outdir2r, "dbg_s1": scr.s1,
                    "dbg_xnrev": scr.xn_rev, "dbg_x2": scr.xcur2,
                    "dbg_rows": scr.rowbounce[0:2, :], "dbg_disp": scr.disp,
                    "dbg_rsout": scr.rs_out, "dbg_eoall": scr.eoall,
                    "dbg_x1": scr.xcur,
                    "dbg_rt": scr.dbgr, "dbg_rt2": scr.dbgr2, "dbg_rt3": scr.dbgr3,
                }[name]
                pairs.append((io.dbg[name], src))
            emit_dram_dump(p, pairs)
        fin_src = {"attn_a": scr.xcur, "ffn1": scr.xcur2, "attn_b": scr.xcur,
                   "ffn2": scr.xcur2, "full": scr.xcur2}[upto]
        finalize(p, io, scr, fin_src)
    return nc



RG8 = [list(range(8))]


_TP_COUNTER = [0]


def emit_transpose_plane(p, cm, src_blk, n_out, dst_blk, which, dt=BF16):
    nc, b = p.nc, p.block
    t_ = "TP"
    ident = cm.identbf if dt == BF16 else cm.ident32
    e = p.ctx
    with ExitStack() as c:
        _TP_COUNTER[0] += 1
        ps = c.enter_context(nc.psum_tensor(which + "_tps" + str(_TP_COUNTER[0]), [128, 2, 1024 if dt == BF16 else 512], dt))
        for nm in ("tr", "ev"):
            p.sem(t_ + nm)
        tb = {nm: p.expect(t_ + nm) for nm in ("tr", "ev")}

        @b.tensor
        def _(tensor):
            for i in range(n_out):
                if i > 1:
                    p.wait(tensor, t_ + "ev", tb["ev"] + i - 1)
                ins = tensor.transpose(ps[:, i % 2, 0:128], src_blk(i), ident[:])
                p.inc(ins, t_ + "tr", 1)

        @b.scalar
        def _(scalar):
            for i in range(n_out):
                p.wait(scalar, t_ + "tr", tb["tr"] + i + 1)
                ins = scalar.copy(dst_blk(i), ps[:, i % 2, 0:128])
                p.inc(ins, t_ + "ev", 1)

        p.barrier()


def emit_ffn(p, io, scr, cm, x_src_dram, x_out_dram, fid):
    nc, b = p.nc, p.block
    e = p.ctx.enter_context
    SC = 40
    TC = 16

    with ExitStack() as fstack:
        ef = fstack.enter_context
        fb = Bufs()
        fb.slotid_rep = ef(nc.sbuf_tensor("f_slotrep" + fid, [128, T], F32))
        fb.gp_rep = ef(nc.sbuf_tensor("f_gprep" + fid, [128, T], F32))
        fb.gpT = ef(nc.sbuf_tensor("f_gpT" + fid, [128, TC], F32))
        fb.iotaslot = ef(nc.sbuf_tensor("f_iotaslot" + fid, [128, SC], F32))

        @b.sync
        def _(sync):
            ins = sync.dma_start(out=fb.iotaslot[:], in_=io.iotaslot)
            p.inc(ins, "Fisl", 16)
            p.wait(sync, "Fisl")

        with ExitStack() as astk:
            ea = astk.enter_context
            fb.xn = ea(nc.sbuf_tensor("ffn_xn" + fid, [128, DC, T], BF16))

            with ExitStack() as c1:
                e1 = c1.enter_context
                lnb = Bufs()
                lnb.xn = fb.xn
                lnb.xbuf = e1(nc.sbuf_tensor("fln_xbuf" + fid, [128, 2, T], F32))
                lnb.sq2 = e1(nc.sbuf_tensor("fln_sq2" + fid, [128, 2, T], F32))
                lnb.rep = e1(nc.sbuf_tensor("fln_rep" + fid, [128, 2, T], F32))
                lnb.rows = e1(nc.sbuf_tensor("fln_rows" + fid, [1, 4, T], F32))
                lnb.ps_stat = e1(nc.psum_tensor("fln_ps_stat" + fid, [1, 2, T], F32))
                emit_ln_phase(p, io, scr, cm, lnb, x_src_dram, "LF", "plain")

            fb.ln2T = ea(nc.sbuf_tensor("ffn_ln2T" + fid, [128, TC, D], BF16))
            emit_transpose_plane(
                p, cm,
                lambda i: fb.xn[:, i % DC, (i // DC) * 128:(i // DC) * 128 + 128],
                TC * DC,
                lambda i: fb.ln2T[:, i // DC, (i % DC) * 128:(i % DC) * 128 + 128],
                "ffnT")

            fb.mask1T = ea(nc.sbuf_tensor("f_mask1T" + fid, [8, T], F32))
            fb.slotid_TM = ea(nc.sbuf_tensor("f_slotTM" + fid, [128, TC], F32))
            fb.iota128rep = ea(nc.sbuf_tensor("f_iota128r" + fid, [128, 128], F32))
            fb.gwt = ea(nc.sbuf_tensor("f_gwt" + fid, [128, DC, 8], BF16))
            fb.gbrep = ea(nc.sbuf_tensor("f_gbrep" + fid, [128, 8], F32))
            fb.iota8c = ea(nc.sbuf_tensor("f_iota8c" + fid, [128, 8], F32))
            fb.iota8m = ea(nc.sbuf_tensor("f_iota8m" + fid, [128, 8], F32))

            @b.gpsimd
            def _(gpsimd):
                ins = gpsimd.dma_start(out=fb.gwt[:], in_=io.gwt.rearrange("k p e -> p k e"),
                                       allow_slow_non_contiguous=True)
                p.inc(ins, "FcldG", 16)

            @b.sync
            def _(sync):
                for ins in [
                    sync.dma_start(out=fb.iota128rep[:], in_=io.iota128rep),
                    sync.dma_start(out=fb.gbrep[:], in_=io.gbrep),
                    sync.dma_start(out=fb.iota8c[:], in_=io.iota8c),
                    sync.dma_start(out=fb.iota8m[:], in_=io.iota8m),
                ]:
                    p.inc(ins, "Fcld", 16)
                p.wait(sync, "Fcld")

            @b.gpsimd
            def _(gpsimd):
                p.wait(gpsimd, "FcldG")
            p.barrier()
            with ExitStack() as c2:
                e2 = c2.enter_context
                rb = Bufs()
                rb.ps_g = e2(nc.psum_tensor("f_ps_g" + fid, [128, 2, 512], F32))
                rb.ps_m = e2(nc.psum_tensor("f_ps_m" + fid, [8, 2, 512], F32))
                rb.lgall = e2(nc.sbuf_tensor("f_lgall" + fid, [128, TC, 8], F32))
                rb.mx8 = e2(nc.sbuf_tensor("f_mx8" + fid, [128, TC, 8], F32))
                rb.mn8 = e2(nc.sbuf_tensor("f_mn8" + fid, [128, TC, 8], F32))
                rb.m1all = e2(nc.sbuf_tensor("f_m1all" + fid, [128, TC, 8], F32))
                rb.ex = e2(nc.sbuf_tensor("f_ex" + fid, [128, TC, 8], F32))
                rb.ex2 = e2(nc.sbuf_tensor("f_ex2" + fid, [128, TC, 8], F32))
                rb.i8ct = e2(nc.sbuf_tensor("f_i8ct" + fid, [128, TC, 8], F32))
                rb.i8mt = e2(nc.sbuf_tensor("f_i8mt" + fid, [128, TC, 8], F32))
                rb.dbg = e2(nc.sbuf_tensor("f_dbgrt" + fid, [128, 64], F32))

                @b.sync
                def _(sync):
                    for src in (io.iota8c, io.iota8m):
                        rep = bass.AP(src.tensor, src.offset, [[8, 128], [0, TC], [1, 8]])
                        dst = rb.i8ct if src is io.iota8c else rb.i8mt
                        ins = sync.dma_start(out=dst[:], in_=rep)
                        p.inc(ins, "Fi8t", 16)

                def glmm_emit(eng, tc):
                    for kc in range(DC):
                        eng.ldweights(fb.xn[:, kc, tc * 128:(tc + 1) * 128])
                        ins = eng.matmul(rb.ps_g[:, tc % 2, 0:8],
                                         fb.xn[:, kc, tc * 128:(tc + 1) * 128],
                                         fb.gwt[:, kc], start=(kc == 0), stop=(kc == DC - 1))
                    return ins

                def gev_emit(eng, tc):
                    return eng.tensor_add(rb.lgall[:, tc], rb.ps_g[:, tc % 2, 0:8],
                                          fb.gbrep[:])

                p.pipe(TC, [
                    dict(name="Fgl", eng="tensor", emit=glmm_emit,
                         after=[("Fgev", lambda i: i - 1)]),
                    dict(name="Fgev", eng="vector", emit=gev_emit,
                         after=[("Fgl", lambda i: i + 1)]),
                ])

                @b.vector
                def _(vector):
                    p.wait(vector, "Fi8t")
                    p.wait(vector, "Fgev")

                    def rt(ins):
                        p.inc(ins, "Frt")
                        p.wait(vector, "Frt")

                    A = rb.mx8
                    rt(vector.tensor_tensor(A[:, :, 0:4], rb.lgall[:, :, 0:4],
                                            rb.lgall[:, :, 4:8], ALU.max))
                    rt(vector.tensor_tensor(A[:, :, 0:2], A[:, :, 0:2],
                                            A[:, :, 2:4], ALU.max))
                    rt(vector.tensor_tensor(A[:, :, 0:1], A[:, :, 0:1],
                                            A[:, :, 1:2], ALU.max))
                    rt(vector.tensor_copy(A[:, :, 1:2], A[:, :, 0:1]))
                    rt(vector.tensor_copy(A[:, :, 2:4], A[:, :, 0:2]))
                    rt(vector.tensor_copy(A[:, :, 4:8], A[:, :, 0:4]))
                    ins = vector.tensor_tensor(rb.ex[:], rb.lgall[:], A[:], ALU.subtract)
                    p.inc(ins, "Fshf")
                    p.wait(vector, "Fshf")
                    rt(vector.tensor_tensor(rb.m1all[:], rb.lgall[:], A[:], ALU.is_equal))
                    rt(vector.tensor_mul(rb.m1all[:], rb.m1all[:], rb.i8mt[:]))
                    rt(vector.tensor_scalar(rb.m1all[:], rb.m1all[:], -1.0, 8.0,
                                            ALU.mult, ALU.add))
                    B = rb.mn8
                    rt(vector.tensor_tensor(B[:, :, 0:4], rb.m1all[:, :, 0:4],
                                            rb.m1all[:, :, 4:8], ALU.min))
                    rt(vector.tensor_tensor(B[:, :, 0:2], B[:, :, 0:2],
                                            B[:, :, 2:4], ALU.min))
                    rt(vector.tensor_tensor(B[:, :, 0:1], B[:, :, 0:1],
                                            B[:, :, 1:2], ALU.min))
                    rt(vector.tensor_copy(B[:, :, 1:2], B[:, :, 0:1]))
                    rt(vector.tensor_copy(B[:, :, 2:4], B[:, :, 0:2]))
                    rt(vector.tensor_copy(B[:, :, 4:8], B[:, :, 0:4]))
                    ins = vector.tensor_tensor(rb.m1all[:], rb.i8ct[:], B[:], ALU.is_equal)
                    p.inc(ins, "Fm1")
                    p.wait(vector, "Fm1")
                    vector.tensor_copy(rb.dbg[:, 0:8], rb.m1all[:, 0])
                    vector.tensor_copy(rb.dbg[:, 8:16], rb.lgall[:, 0])
                    ins = vector.tensor_copy(rb.dbg[:, 16:24], A[:, 0])
                    p.inc(ins, "Fdbgv")

                @b.scalar
                def _(scalar):
                    p.wait(scalar, "Fshf")
                    ins = scalar.activation(rb.ex2[:], rb.ex[:], ACT.Exp)
                    p.inc(ins, "Fex")

                @b.vector
                def _(vector):
                    p.wait(vector, "Fex")

                    def rt(ins):
                        p.inc(ins, "Fsx")
                        p.wait(vector, "Fsx")

                    rt(vector.tensor_add(rb.ex2[:, :, 0:4], rb.ex2[:, :, 0:4],
                                         rb.ex2[:, :, 4:8]))
                    rt(vector.tensor_add(rb.ex2[:, :, 0:2], rb.ex2[:, :, 0:2],
                                         rb.ex2[:, :, 2:4]))
                    rt(vector.tensor_add(rb.ex2[:, :, 0:1], rb.ex2[:, :, 0:1],
                                         rb.ex2[:, :, 1:2]))
                    ins = vector.reciprocal(fb.gpT[:, 0:TC], rb.ex2[:, :, 0:1])
                    p.inc(ins, "Fgp")

                def mtr_emit(eng, tc):
                    par = tc % 2
                    return eng.transpose(rb.ps_m[:8, par, 0:128], rb.m1all[:, tc],
                                         cm.ident32[:])

                def mev_emit(eng, tc):
                    par = tc % 2
                    return eng.copy(fb.mask1T[:, tc * 128:(tc + 1) * 128],
                                    rb.ps_m[:8, par, 0:128])

                @b.tensor
                def _(tensor):
                    p.wait(tensor, "Fm1")

                p.pipe(TC, [
                    dict(name="Fmtr", eng="tensor", emit=mtr_emit,
                         after=[("Fmev", lambda i: i - 1)]),
                    dict(name="Fmev", eng="scalar", emit=mev_emit,
                         after=[("Fmtr", lambda i: i + 1)]),
                ])
                p.barrier()

                @b.sync
                def _(sync):
                    ins = sync.dma_start(out=scr.dbgr2, in_=rb.dbg[:])
                    p.inc(ins, "Fdbg2", 16)
                    ins = sync.dma_start(out=scr.dbgr[0:8, :], in_=fb.mask1T[:])
                    p.inc(ins, "Fdbg2", 16)
                    p.wait(sync, "Fdbg2")
                p.barrier()

            with ExitStack() as c3:
                e3 = c3.enter_context
                pb = Bufs()
                pb.zrow = e3(nc.sbuf_tensor("f_zrow" + fid, [8, T], F32))
                pb.csum = e3(nc.sbuf_tensor("f_csum" + fid, [8, T], F32))
                pb.prefm = e3(nc.sbuf_tensor("f_prefm" + fid, [64, 8], F32))
                pb.call64 = e3(nc.sbuf_tensor("f_call64" + fid, [64, 1], F32))
                pb.ones8 = e3(nc.sbuf_tensor("f_ones8" + fid, [8, 1], F32))
                pb.i8col = e3(nc.sbuf_tensor("f_i8col" + fid, [8, 1], F32))
                pb.pref = e3(nc.sbuf_tensor("f_pref" + fid, [8, 1], F32))
                pb.rows = e3(nc.sbuf_tensor("f_rows" + fid, [1, 3, T], F32))
                pb.ps_r = e3(nc.psum_tensor("f_ps_r" + fid, [8, 2, T], F32))

                @b.sync
                def _(sync):
                    for ins in [sync.dma_start(out=pb.prefm[:], in_=io.prefmask),
                                sync.dma_start(out=pb.ones8[:], in_=io.ones8),
                                sync.dma_start(out=pb.i8col[:], in_=io.iota8col)]:
                        p.inc(ins, "Fcld2", 16)
                    p.wait(sync, "Fcld2")

                @b.vector
                def _(vector):
                    p.wait(vector, "Fcld2")
                    vector.memset(pb.zrow[:], 0.0)
                    ins = vector.tensor_tensor_scan(pb.csum[:], fb.mask1T[:], pb.zrow[:],
                                                    0.0, ALU.add, ALU.add)
                    p.inc(ins, "Fcs")

                @b.sync
                def _(sync):
                    p.wait(sync, "Fcs")
                    ins = sync.dma_start(out=scr.cc_in, in_=pb.csum[:, T - 1:T])
                    p.inc(ins, "Fccw", 16)
                    p.wait(sync, "Fccw")

                @b.gpsimd
                def _(gpsimd):
                    p.wait(gpsimd, "Fccw")
                    ins = gpsimd.collective_compute(
                        "AllGather", ALU.bypass, replica_groups=RG8,
                        ins=[scr.cc_in], outs=[scr.cc_out])
                    p.inc(ins, "Fcc")

                @b.sync
                def _(sync):
                    p.wait(sync, "Fcc")
                    ins = sync.dma_start(out=pb.call64[:], in_=scr.cc_out)
                    p.inc(ins, "Fccr", 16)

                @b.tensor
                def _(tensor):
                    p.wait(tensor, "Fccr")
                    ins = tensor.matmul(pb.ps_r[:8, 0, 0:1], pb.prefm[:], pb.call64[:],
                                        start=True, stop=True)
                    p.inc(ins, "Fpmm")

                @b.scalar
                def _(scalar):
                    p.wait(scalar, "Fpmm")
                    ins = scalar.copy(pb.pref[:], pb.ps_r[:8, 0, 0:1])
                    p.inc(ins, "Fpev")

                @b.vector
                def _(vector):
                    p.wait(vector, "Fpev")
                    vector.tensor_scalar(pb.csum[:], pb.csum[:], pb.pref[:, 0:1], -1.0,
                                         ALU.add, ALU.add)
                    ins = vector.tensor_mul(pb.csum[:], pb.csum[:], fb.mask1T[:])
                    p.inc(ins, "Fmpos")

                @b.tensor
                def _(tensor):
                    p.wait(tensor, "Fmpos")
                    for tb2 in range(NTB):
                        sl = slice(tb2 * 512, (tb2 + 1) * 512)
                        ins = tensor.matmul(pb.ps_r[0:1, 0, sl], pb.ones8[:], pb.csum[:, sl],
                                            start=True, stop=True)
                        ins = tensor.matmul(pb.ps_r[0:1, 1, sl], pb.i8col[:], fb.mask1T[:, sl],
                                            start=True, stop=True)
                    p.inc(ins, "Frmm")

                @b.vector
                def _(vector):
                    p.wait(vector, "Frmm")
                    pos = pb.rows[0:1, 0]
                    slot = pb.rows[0:1, 1]
                    keep = pb.rows[0:1, 2]
                    vector.tensor_copy(pos, pb.ps_r[0:1, 0])
                    vector.tensor_copy(slot, pb.ps_r[0:1, 1])
                    vector.tensor_single_scalar(keep, pos, float(CAP), ALU.is_lt)
                    vector.tensor_scalar_mul(slot, slot, float(CAP))
                    vector.tensor_add(slot, slot, pos)
                    vector.tensor_single_scalar(slot, slot, -6400.0, ALU.add)
                    vector.tensor_mul(slot, slot, keep)
                    ins = vector.tensor_single_scalar(slot, slot, 6400.0, ALU.add)
                    p.inc(ins, "Fslot")

                @b.sync
                def _(sync):
                    p.wait(sync, "Fslot")
                    ins = sync.dma_start(out=scr.dbgr[8:16, :], in_=pb.csum[:])
                    p.inc(ins, "Fdbg3", 16)
                    ins = sync.dma_start(out=scr.dbgr3[:, 0:1], in_=pb.call64[:],
                                         allow_slow_non_contiguous=True)
                    p.inc(ins, "Fdbg3", 16)
                    ins = sync.dma_start(out=scr.dbgr3[0:8, 1:2], in_=pb.pref[:],
                                         allow_slow_non_contiguous=True)
                    p.inc(ins, "Fdbg3", 16)
                    p.wait(sync, "Fdbg3")
                    ins = sync.dma_start(out=scr.rowbounce[0:1, :], in_=pb.rows[0:1, 1])
                    p.inc(ins, "Frw", 16)
                    gp_dst = bass.AP(scr.rowbounce.tensor,
                                     scr.rowbounce.offset + scr.rowbounce.ap[0][0],
                                     [[1, 128], [128, TC]])
                    ins = sync.dma_start(out=gp_dst, in_=fb.gpT[:],
                                         allow_slow_non_contiguous=True)
                    p.inc(ins, "Frw", 16)
                    p.wait(sync, "Frw")
                    r0 = scr.rowbounce[0:1, :]
                    src = bass.AP(r0.tensor, r0.offset, [[0, 128], [1, T]])
                    ins = sync.dma_start(out=fb.slotid_rep[:], in_=src)
                    p.inc(ins, "Frr", 16)
                    r1 = scr.rowbounce[1:2, :]
                    src = bass.AP(r1.tensor, r1.offset, [[0, 128], [1, T]])
                    ins = sync.dma_start(out=fb.gp_rep[:], in_=src)
                    p.inc(ins, "Frr", 16)
                    src = bass.AP(r0.tensor, r0.offset, [[1, 128], [128, TC]])
                    ins = sync.dma_start(out=fb.slotid_TM[:], in_=src,
                                         allow_slow_non_contiguous=True)
                    p.inc(ins, "Frr", 16)
                    p.wait(sync, "Frr")
                p.barrier()

            with ExitStack() as c4:
                e4 = c4.enter_context
                db = Bufs()
                db.sm = e4(nc.sbuf_tensor("f_sm" + fid, [128, 2, TC, 128], BF16))
                db.dbf = e4(nc.sbuf_tensor("f_dbf" + fid, [128, 2, D], BF16))
                db.ps_d = e4(nc.psum_tensor("f_ps_d" + fid, [128, 2, D], F32))

                def smb_emit(eng, sc):
                    for tc in range(TC):
                        ins = eng.tensor_scalar(db.sm[:, sc % 2, tc], fb.iota128rep[:],
                                                128.0 * sc, fb.slotid_TM[:, tc:tc + 1],
                                                ALU.add, ALU.is_equal)
                    return ins

                def dmm_emit(eng, sc):
                    par = sc % 2
                    for tc in range(TC):
                        eng.ldweights(db.sm[:, par, tc])
                        for hb in range(2):
                            ins = eng.matmul(db.ps_d[:, par, hb * 512:(hb + 1) * 512],
                                             db.sm[:, par, tc],
                                             fb.ln2T[:, tc, hb * 512:(hb + 1) * 512],
                                             start=(tc == 0), stop=(tc == TC - 1))
                    return ins

                def dev_emit(eng, sc):
                    return eng.copy(db.dbf[:, sc % 2], db.ps_d[:, sc % 2])

                def dst_emit(eng, sc):
                    return [eng.dma_start(out=scr.disp[sc * 128:(sc + 1) * 128, :],
                                          in_=db.dbf[:, sc % 2])]

                p.pipe(SC, [
                    dict(name="Fsmb", eng="vector", emit=smb_emit,
                         after=[("Fdmm", lambda i: i - 1)]),
                    dict(name="Fdmm", eng="tensor", emit=dmm_emit,
                         after=[("Fsmb", lambda i: i + 1), ("Fdev", lambda i: i - 1)]),
                    dict(name="Fdev", eng="scalar", emit=dev_emit,
                         after=[("Fdmm", lambda i: i + 1), ("Fdst", lambda i: i - 1)]),
                    dict(name="Fdst", eng="gpsimd", emit=dst_emit, dma=lambda i: 1,
                         after=[("Fdev", lambda i: i + 1)]),
                ])
                p.barrier()

        @b.gpsimd
        def _(gpsimd):
            ins = gpsimd.collective_compute(
                "ReduceScatter", ALU.add, replica_groups=RG8,
                ins=[scr.disp], outs=[scr.rs_out])
            p.inc(ins, "Frs")

        @b.sync
        def _(sync):
            p.wait(sync, "Frs")
            p.inc(sync.nop(), "Frsw")
        p.barrier()

        with ExitStack() as estack:
            ee = estack.enter_context
            fb.dispF = ee(nc.sbuf_tensor("f_dispF" + fid, [128, DC, CAP], BF16))
            fb.hbuf = ee(nc.sbuf_tensor("f_hbuf" + fid, [128, 32, CAP], BF16))
            fb.w2res = ee(nc.sbuf_tensor("f_w2res" + fid, [128, 32, D], BF16))
            fb.b1c = ee(nc.sbuf_tensor("f_b1c" + fid, [128, 32], F32))
            fb.b2rep = ee(nc.sbuf_tensor("f_b2rep" + fid, [128, D], F32))

            @b.gpsimd
            def _(gpsimd):
                ins = gpsimd.dma_start(out=fb.w2res[:], in_=io.w2t)
                p.inc(ins, "Fw2ld", 16)

            @b.sync
            def _(sync):
                ins = sync.dma_start(out=fb.b1c[:], in_=io.b1c)
                p.inc(ins, "Fbld", 16)
                ins = sync.dma_start(out=fb.b2rep[:], in_=io.b2rep)
                p.inc(ins, "Fbld", 16)
                p.wait(sync, "Fbld")

            with ExitStack() as c5:
                e5 = c5.enter_context
                tb5 = Bufs()
                tb5.rsin = e5(nc.sbuf_tensor("f_rsin" + fid, [128, 5, D], BF16))

                @b.sync
                def _(sync):
                    ins = sync.dma_start(out=tb5.rsin[:],
                                         in_=scr.rs_out.rearrange("(c p) d -> p c d", p=128))
                    p.inc(ins, "Frsl", 16)
                    p.wait(sync, "Frsl")
                p.barrier()
                emit_transpose_plane(
                    p, cm,
                    lambda i: tb5.rsin[:, i % 5, (i // 5) * 128:(i // 5) * 128 + 128],
                    5 * DC,
                    lambda i: fb.dispF[:, i // 5, (i % 5) * 128:(i % 5) * 128 + 128],
                    "ffnD")

            with ExitStack() as c6:
                e6 = c6.enter_context
                hb = Bufs()
                hb.w1 = e6(nc.sbuf_tensor("f_w1b" + fid, [128, 2, DC, 128], BF16))
                hb.ps_h = e6(nc.psum_tensor("f_ps_h" + fid, [128, 2, 1024], F32))

                def w1ld_emit(eng, fc):
                    return [eng.dma_start(out=hb.w1[:, fc % 2], in_=io.w1t[fc])]

                def hmm_emit(eng, fc):
                    par = fc % 2
                    for kc in range(DC):
                        eng.ldweights(hb.w1[:, par, kc])
                        ins = eng.matmul(hb.ps_h[:, par, 0:512], hb.w1[:, par, kc],
                                         fb.dispF[:, kc, 0:512],
                                         start=(kc == 0), stop=(kc == DC - 1))
                        ins = eng.matmul(hb.ps_h[:, par, 512:512 + 128], hb.w1[:, par, kc],
                                         fb.dispF[:, kc, 512:CAP],
                                         start=(kc == 0), stop=(kc == DC - 1))
                    return ins

                def hev_emit(eng, fc):
                    par = fc % 2
                    eng.activation(fb.hbuf[:, fc, 0:512], hb.ps_h[:, par, 0:512],
                                   ACT.Gelu_apprx_tanh, bias=fb.b1c[:, fc:fc + 1])
                    ins = eng.activation(fb.hbuf[:, fc, 512:CAP], hb.ps_h[:, par, 512:512 + 128],
                                         ACT.Gelu_apprx_tanh, bias=fb.b1c[:, fc:fc + 1])
                    return ins

                p.pipe(32, [
                    dict(name="Fw1", eng="gpsimd", emit=w1ld_emit, dma=lambda i: 1,
                         after=[("Fhmm", lambda i: i - 1)]),
                    dict(name="Fhmm", eng="tensor", emit=hmm_emit,
                         after=[("Fw1", lambda i: i + 1), ("Fhev", lambda i: i - 1)]),
                    dict(name="Fhev", eng="scalar", emit=hev_emit,
                         after=[("Fhmm", lambda i: i + 1)]),
                ])
                p.barrier()

            with ExitStack() as c7:
                e7 = c7.enter_context
                eb = Bufs()
                eb.ps_e = e7(nc.psum_tensor("f_ps_e" + fid, [128, 2, 1024], F32))
                eb.ebf = e7(nc.sbuf_tensor("f_ebf" + fid, [128, 2, D], BF16))

                def eomm_emit(eng, sc):
                    par = sc % 2
                    for fc in range(32):
                        eng.ldweights(fb.hbuf[:, fc, sc * 128:(sc + 1) * 128])
                        for hb2 in range(2):
                            ins = eng.matmul(eb.ps_e[:, par, hb2 * 512:(hb2 + 1) * 512],
                                             fb.hbuf[:, fc, sc * 128:(sc + 1) * 128],
                                             fb.w2res[:, fc, hb2 * 512:(hb2 + 1) * 512],
                                             start=(fc == 0), stop=(fc == 31))
                    return ins

                def eoev_emit(eng, sc):
                    par = sc % 2
                    return eng.tensor_add(eb.ebf[:, par], eb.ps_e[:, par], fb.b2rep[:])

                def eost_emit(eng, sc):
                    return [eng.dma_start(out=scr.eo[sc * 128:(sc + 1) * 128, :],
                                          in_=eb.ebf[:, sc % 2])]

                p.pipe(5, [
                    dict(name="Feomm", eng="tensor", emit=eomm_emit,
                         after=[("Fw2ld", lambda i: 0), ("Feoev", lambda i: i - 1)]),
                    dict(name="Feoev", eng="vector", emit=eoev_emit,
                         after=[("Feomm", lambda i: i + 1), ("Feost", lambda i: i - 1)]),
                    dict(name="Feost", eng="gpsimd", emit=eost_emit, dma=lambda i: 1,
                         after=[("Feoev", lambda i: i + 1)]),
                ])
                p.barrier()

        @b.gpsimd
        def _(gpsimd):
            ins = gpsimd.collective_compute(
                "AllGather", ALU.bypass, replica_groups=RG8,
                ins=[scr.eo], outs=[scr.eoall])
            p.inc(ins, "Fag")

        @b.sync
        def _(sync):
            p.wait(sync, "Fag")
            p.inc(sync.nop(), "Fagw")
        p.barrier()

        with ExitStack() as c8:
            e8 = c8.enter_context
            cb = Bufs()
            cb.eoall = e8(nc.sbuf_tensor("f_eoall" + fid, [128, SC, D], BF16))
            cb.cw = e8(nc.sbuf_tensor("f_cw" + fid, [128, 2, 512], BF16))
            cb.xres = e8(nc.sbuf_tensor("f_xres" + fid, [128, 2, DC, 512], F32))
            cb.outb = e8(nc.sbuf_tensor("f_outb" + fid, [128, 2, DC, 512], F32))
            cb.ps_c = e8(nc.psum_tensor("f_ps_cb" + fid, [128, DC, 512], F32))

            @b.sync
            def _(sync):
                ins = sync.dma_start(out=cb.eoall[:],
                                     in_=scr.eoall.rearrange("(c p) d -> p c d", p=128))
                p.inc(ins, "Feol", 16)
                p.wait(sync, "Feol")
            p.barrier()

            NCB = NTB * SC

            def cwb_emit(eng, st):
                tb2, sc = divmod(st, SC)
                sl = slice(tb2 * 512, (tb2 + 1) * 512)
                return eng.scalar_tensor_tensor(
                    cb.cw[:, st % 2], fb.slotid_rep[:, sl],
                    fb.iotaslot[:, sc:sc + 1], fb.gp_rep[:, sl],
                    ALU.is_equal, ALU.mult)

            def cmm_emit(eng, st):
                tb2, sc = divmod(st, SC)
                for dc in range(DC):
                    eng.ldweights(cb.eoall[:, sc, dc * 128:(dc + 1) * 128])
                    ins = eng.matmul(cb.ps_c[:, dc], cb.eoall[:, sc, dc * 128:(dc + 1) * 128],
                                     cb.cw[:, st % 2], start=(sc == 0), stop=(sc == SC - 1))
                return ins

            def xld_emit(eng, st):
                tb2, sc = divmod(st, SC)
                if sc == 0:
                    sl = slice(tb2 * 512, (tb2 + 1) * 512)
                    return [eng.dma_start(
                        out=cb.xres[:, tb2 % 2],
                        in_=x_src_dram[:, sl].rearrange("(c p) t -> p c t", p=128))]
                return []

            def cev_emit(eng, st):
                tb2, sc = divmod(st, SC)
                if sc != SC - 1:
                    return None
                for dc in range(DC):
                    ins = eng.tensor_add(cb.outb[:, tb2 % 2, dc], cb.ps_c[:, dc],
                                         cb.xres[:, tb2 % 2, dc])
                return ins

            def cst_emit(eng, st):
                tb2, sc = divmod(st, SC)
                if sc != SC - 1:
                    return []
                sl = slice(tb2 * 512, (tb2 + 1) * 512)
                return [eng.dma_start(
                    out=x_out_dram[:, sl].rearrange("(c p) t -> p c t", p=128),
                    in_=cb.outb[:, tb2 % 2])]

            p.pipe(NCB, [
                dict(name="Fxld", eng="sync", emit=xld_emit,
                     dma=lambda st: 1 if st % SC == 0 else 0,
                     after=[("Fcev", lambda st: SC * (st // SC - 1))]),
                dict(name="Fcwb", eng="vector", emit=cwb_emit,
                     after=[("Fcmm", lambda st: st - 1)]),
                dict(name="Fcmm", eng="tensor", emit=cmm_emit,
                     after=[("Fcwb", lambda st: st + 1),
                            ("Fcev", lambda st: SC * (st // SC))]),
                dict(name="Fcev", eng="vector", emit=cev_emit,
                     after=[("Fcmm", lambda st: st + 1),
                            ("Fxld", lambda st: (st // SC) * SC + 1),
                            ("Fcst", lambda st: SC * (st // SC - 1))]),
                dict(name="Fcst", eng="gpsimd", emit=cst_emit,
                     dma=lambda st: 1 if st % SC == SC - 1 else 0,
                     after=[("Fcev", lambda st: st + 1)]),
            ])
            p.barrier()



SCALING = 0.125


def emit_attn_b(p, io, scr, cm, x_src_dram, x_out_dram):
    nc, b = p.nc, p.block
    TC = 16
    NH = 16
    with ExitStack() as astk:
        ea = astk.enter_context
        ab = Bufs()
        ab.attnout = ea(nc.sbuf_tensor("b_attnout", [128, 8, T], BF16))
        ab.vT = ea(nc.sbuf_tensor("b_vT", [128, TC, D], BF16))
        ab.zp = ea(nc.sbuf_tensor("b_zp", [16, T], BF16))
        ab.bOc = ea(nc.sbuf_tensor("b_bOc", [128, 8], F32))

        with ExitStack() as s1_:
            e1 = s1_.enter_context
            ab.xn = e1(nc.sbuf_tensor("b_xn", [128, DC, T], BF16))

            with ExitStack() as c1:
                e2 = c1.enter_context
                lnb = Bufs()
                lnb.xn = ab.xn
                lnb.xbuf = e2(nc.sbuf_tensor("bln_xbuf", [128, 2, T], F32))
                lnb.sq2 = e2(nc.sbuf_tensor("bln_sq2", [128, 2, T], F32))
                lnb.rep = e2(nc.sbuf_tensor("bln_rep", [128, 2, T], F32))
                lnb.rows = e2(nc.sbuf_tensor("bln_rows", [1, 4, T], F32))
                lnb.ps_stat = e2(nc.psum_tensor("bln_ps_stat", [1, 2, T], F32))
                emit_ln_phase(p, io, scr, cm, lnb, x_src_dram, "LB", "plain")

            with ExitStack() as c2:
                e3 = c2.enter_context
                zb = Bufs()
                zb.watt = e3(nc.sbuf_tensor("b_watt", [128, DC, 16], BF16))
                zb.zsb = e3(nc.sbuf_tensor("b_zsb", [16, T], F32))
                zb.zex = e3(nc.sbuf_tensor("b_zex", [16, T], F32))
                zb.cols = e3(nc.sbuf_tensor("b_cols", [16, 4], F32))
                zb.zpad = e3(nc.sbuf_tensor("b_zpad", [16, 128], BF16))
                zb.ps_z = e3(nc.psum_tensor("b_ps_z", [16, T], F32))

                @b.gpsimd
                def _(gpsimd):
                    ins = gpsimd.dma_start(out=zb.watt[:], in_=io.watt)
                    p.inc(ins, "Bwatt", 16)

                @b.tensor
                def _(tensor):
                    p.wait(tensor, "Bwatt")
                    for tb2 in range(NTB):
                        sl = slice(tb2 * 512, (tb2 + 1) * 512)
                        for kc in range(DC):
                            tensor.ldweights(zb.watt[:, kc])
                            ins = tensor.matmul(zb.ps_z[:16, sl], zb.watt[:, kc],
                                                ab.xn[:, kc, sl],
                                                start=(kc == 0), stop=(kc == DC - 1))
                    p.inc(ins, "Bzmm")

                @b.scalar
                def _(scalar):
                    p.wait(scalar, "Bzmm")
                    ins = scalar.copy(zb.zsb[:], zb.ps_z[:16, :])
                    p.inc(ins, "Bzev")

                @b.vector
                def _(vector):
                    p.wait(vector, "Bzev")
                    ins = vector.reduce_max(zb.cols[:, 0:1], zb.zsb[:], axis=AX)
                    p.inc(ins, "Bmxr")
                    p.wait(vector, "Bmxr")
                    ins = vector.tensor_scalar_mul(zb.cols[:, 1:2], zb.cols[:, 0:1], -SCALING)
                    p.inc(ins, "Bmx")

                @b.scalar
                def _(scalar):
                    p.wait(scalar, "Bmx")
                    ins = scalar.activation(zb.zex[:], zb.zsb[:], ACT.Exp,
                                            scale=SCALING, bias=zb.cols[:, 1:2],
                                            accum_out=zb.cols[:, 2:3])
                    p.inc(ins, "Bex")

                @b.vector
                def _(vector):
                    p.wait(vector, "Bex")
                    ins = vector.reciprocal(zb.cols[:, 3:4], zb.cols[:, 2:3])
                    p.inc(ins, "Brcp")
                    p.wait(vector, "Brcp")
                    vector.tensor_scalar_mul(ab.zp[:], zb.zex[:], zb.cols[:, 3:4])
                    ins = vector.memset(zb.zpad[:], 0.0)
                    p.inc(ins, "Bzp")

                @b.sync
                def _(sync):
                    p.wait(sync, "Bzp")
                    ins = sync.dma_start(out=scr.wp[:, 0:127], in_=zb.zpad[:, 0:127])
                    p.inc(ins, "Bwp", 16)
                    ins = sync.dma_start(out=scr.wp[:, 127:127 + T], in_=ab.zp[:])
                    p.inc(ins, "Bwp", 16)
                    p.wait(sync, "Bwp")
                p.barrier()

            with ExitStack() as c3:
                e4 = c3.enter_context
                vb = Bufs()
                vb.wvt = e4(nc.sbuf_tensor("b_wvt", [128, DC, D], BF16))
                vb.ps_v = e4(nc.psum_tensor("b_ps_v", [128, 2, D], F32))

                @b.gpsimd
                def _(gpsimd):
                    ins = gpsimd.dma_start(out=vb.wvt[:], in_=io.wvtt)
                    p.inc(ins, "Bwvt", 16)

                def vmm_emit(eng, tc):
                    par = tc % 2
                    for kc in range(DC):
                        eng.ldweights(ab.xn[:, kc, tc * 128:(tc + 1) * 128])
                        for hb2 in range(2):
                            ins = eng.matmul(vb.ps_v[:, par, hb2 * 512:(hb2 + 1) * 512],
                                             ab.xn[:, kc, tc * 128:(tc + 1) * 128],
                                             vb.wvt[:, kc, hb2 * 512:(hb2 + 1) * 512],
                                             start=(kc == 0), stop=(kc == DC - 1))
                    return ins

                def vev_emit(eng, tc):
                    return eng.copy(ab.vT[:, tc], vb.ps_v[:, tc % 2])

                p.pipe(TC, [
                    dict(name="Bvmm", eng="tensor", emit=vmm_emit,
                         after=[("Bwvt", lambda i: 0), ("Bvev", lambda i: i - 1)]),
                    dict(name="Bvev", eng="scalar", emit=vev_emit,
                         after=[("Bvmm", lambda i: i + 1)]),
                ])
                p.barrier()

        with ExitStack() as c4:
            e5 = c4.enter_context
            rb = Bufs()
            rb.srev = e5(nc.sbuf_tensor("b_srev", [128, 2, T], BF16))
            rb.sd = e5(nc.sbuf_tensor("b_sd", [128, 2, NH, 128], BF16))
            rb.ps_s = e5(nc.psum_tensor("b_ps_s", [128, T], F32))
            rb.ps_r = e5(nc.psum_tensor("b_ps_r", [128, 2, 512], F32))

            def sld_emit(eng, st):
                h = st
                src = bass.AP(scr.wp.tensor, scr.wp.offset + h * 2304,
                              [[1, 128], [128, NH], [1, 128]])
                return [eng.dma_start(out=rb.srev[:, h % 2], in_=src)]

            def sj_emit(eng, st):
                h = st
                par = h % 2
                for tb2 in range(NTB):
                    sl = slice(tb2 * 512, (tb2 + 1) * 512)
                    eng.ldweights(cm.antibf[:])
                    ins = eng.matmul(rb.ps_s[:, sl], cm.antibf[:], rb.srev[:, par, sl],
                                     start=True, stop=True)
                return ins

            def sev_emit(eng, st):
                h = st
                return eng.copy(rb.sd[:, h % 2].rearrange("p n q -> p (n q)"), rb.ps_s[:])

            p.sem("Brmo")
            p.sem("Bevm")
            b_brmo = p.expect("Brmo")
            b_bevm = p.expect("Bevm")

            def roll_emit(eng, st):
                h = st - 1
                if h < 0 or h >= NH:
                    return None
                par = h % 2
                cpos = (h % 2) * 64
                for mo in range(NH):
                    g = h * NH + mo
                    if g >= 2:
                        eng.wait_ge(p.sem("Bevm")[0], b_bevm + g - 1)
                    pr = rb.ps_r[:, mo % 2]
                    for mi in range(mo + 1):
                        eng.ldweights(ab.vT[:, mi, h * 64:(h + 1) * 64],
                                      tile_position=(0, cpos))
                        ins = eng.matmul(
                            pr[cpos:cpos + 64, 0:128],
                            ab.vT[:, mi, h * 64:(h + 1) * 64],
                            rb.sd[:, par, mo - mi],
                            start=(mi == 0), stop=(mi == mo),
                            tile_position=(0, cpos))
                    p.inc(ins, "Brmo", 1)
                return None

            def rev_emit(eng, st):
                h = st - 1
                if h < 0 or h >= NH:
                    return None
                hp = h // 2
                cpos = (h % 2) * 64
                for mo in range(NH):
                    g = h * NH + mo
                    eng.wait_ge(p.sem("Brmo")[0], b_brmo + g + 1)
                    ins = eng.copy(
                        ab.attnout[cpos:cpos + 64, hp, mo * 128:(mo + 1) * 128],
                        rb.ps_r[cpos:cpos + 64, mo % 2, 0:128])
                    p.inc(ins, "Bevm", 1)
                return None

            p.pipe(NH + 1, [
                dict(name="Bsld", eng="sync",
                     emit=lambda eng, st: sld_emit(eng, st) if st < NH else [],
                     dma=lambda st: 1 if st < NH else 0,
                     after=[("Bsj", lambda st: st - 1)]),
                dict(name="Bsj", eng="tensor",
                     emit=lambda eng, st: sj_emit(eng, st) if st < NH else None,
                     after=[("Bsld", lambda st: st + 1), ("Bsev", lambda st: st)]),
                dict(name="Bsev", eng="scalar",
                     emit=lambda eng, st: sev_emit(eng, st) if st < NH else None,
                     after=[("Bsj", lambda st: st + 1), ("Broll", lambda st: st)]),
                dict(name="Broll", eng="tensor", emit=roll_emit,
                     after=[("Bsev", lambda st: st)]),
                dict(name="Brev", eng="scalar", emit=rev_emit,
                     after=[]),
            ])
            p.barrier()

        with ExitStack() as c5:
            e6 = c5.enter_context
            wb = Bufs()
            wb.wot = e6(nc.sbuf_tensor("b_wot", [128, 8, 8, 128], BF16))
            wb.xres = e6(nc.sbuf_tensor("b_xres", [128, 2, 512], F32))
            wb.obuf = e6(nc.sbuf_tensor("b_obuf", [128, 2, 512], F32))
            wb.ps_o = e6(nc.psum_tensor("b_ps_o", [128, 2, 512], F32))

            @b.gpsimd
            def _(gpsimd):
                ins = gpsimd.dma_start(out=wb.wot[:], in_=io.wott)
                p.inc(ins, "Bwot", 16)

            @b.sync
            def _(sync):
                ins = sync.dma_start(out=ab.bOc[:], in_=io.bOc)
                p.inc(ins, "BbO", 16)
                p.wait(sync, "BbO")

            NOS = DC * NTB

            def oxld_emit(eng, st):
                do, tb2 = divmod(st, NTB)
                sl = slice(tb2 * 512, (tb2 + 1) * 512)
                return [eng.dma_start(out=wb.xres[:, st % 2],
                                      in_=x_src_dram[do * 128:(do + 1) * 128, sl])]

            def omm_emit(eng, st):
                do, tb2 = divmod(st, NTB)
                sl = slice(tb2 * 512, (tb2 + 1) * 512)
                for hc in range(8):
                    eng.ldweights(wb.wot[:, hc, do])
                    ins = eng.matmul(wb.ps_o[:, st % 2], wb.wot[:, hc, do],
                                     ab.attnout[:, hc, sl],
                                     start=(hc == 0), stop=(hc == 7))
                return ins

            def oev_emit(eng, st):
                do, tb2 = divmod(st, NTB)
                return eng.scalar_tensor_tensor(wb.obuf[:, st % 2], wb.ps_o[:, st % 2],
                                                ab.bOc[:, do:do + 1], wb.xres[:, st % 2],
                                                ALU.add, ALU.add)

            def ost_emit(eng, st):
                do, tb2 = divmod(st, NTB)
                sl = slice(tb2 * 512, (tb2 + 1) * 512)
                return [eng.dma_start(out=x_out_dram[do * 128:(do + 1) * 128, sl],
                                      in_=wb.obuf[:, st % 2])]

            p.pipe(NOS, [
                dict(name="Boxld", eng="sync", emit=oxld_emit, dma=lambda st: 1,
                     after=[("Boev", lambda st: st - 1)]),
                dict(name="Bomm", eng="tensor", emit=omm_emit,
                     after=[("Bwot", lambda st: 0), ("Boev", lambda st: st - 1)]),
                dict(name="Boev", eng="vector", emit=oev_emit,
                     after=[("Bomm", lambda st: st + 1), ("Boxld", lambda st: st + 1),
                            ("Bost", lambda st: st - 1)]),
                dict(name="Bost", eng="gpsimd", emit=ost_emit, dma=lambda st: 1,
                     after=[("Boev", lambda st: st + 1)]),
            ])

            p.barrier()


def kernel(**inputs):
    inp = {k: np.asarray(v) for k, v in inputs.items()}
    nc = build_upto("full", [])
    maps = [host_prep(i, inp) for i in range(8)]
    res = run_bass_kernel_spmd(nc, maps, core_ids=list(range(8)))
    out = np.stack([np.asarray(res.results[i]["out"]).T for i in range(8)], axis=0)
    return np.ascontiguousarray(out.astype(np.float32))
